# version 05
# baseline (speedup 1.0000x reference)
import contextlib
import numpy as np
import ml_dtypes
import concourse.bass as bass
import concourse.mybir as mybir
from concourse.bass_utils import run_bass_kernel_spmd

F32 = mybir.dt.float32
BF16 = mybir.dt.bfloat16
U32 = mybir.dt.uint32
AF = mybir.ActivationFunctionType
ALU = mybir.AluOpType
AX = mybir.AxisListType

NCORE = 8
B, N, D = 4, 4096, 1024
H, DH = 16, 64
NOWN = 2048
QB = 512
NQB = NOWN // QB
KT = N // 128
WIN = 256
NWIN = NOWN // WIN
DEPTH = 6
DI = 2730
DIP = 2816
NHT = DIP // 128
HHALF = NHT // 2
NEG = -1.0e30
HG = H * 65
ENGS = ("tensor", "vector", "scalar", "sync", "gpsimd")
BF = ml_dtypes.bfloat16

STOP_AFTER = None
SIM_SKIP_UNINIT_DUMP = False
MB_SELF = False



def host_const_tables():
    t = {}
    pos = np.arange(N, dtype=np.float64)
    inv_freq = 10000.0 ** (-(np.arange(0, DH, 2, dtype=np.float64) / DH))
    theta = pos[None, :] * inv_freq[:, None]
    cosk, sink = np.cos(theta), np.sin(theta)
    t["ocos"] = np.tile(cosk, (4, 1)).astype(np.float32)
    t["osin"] = np.concatenate([-sink, sink, -sink, sink], 0).astype(np.float32)

    half = np.arange(0, DH, 2, dtype=np.float64)
    inv2 = 1.0 / 10000.0 ** (half / DH)
    tt = np.arange(N, dtype=np.float64)
    fr = tt[None, :] * inv2[:, None]
    base_scale = (half + 0.4 * DH) / (1.4 * DH)
    power = (tt - N // 2) / (WIN // 2)
    scale = base_scale[:, None] ** power[None, :]
    cos64 = np.concatenate([np.cos(fr)] * 2, 0)
    ssin64 = np.concatenate([-np.sin(fr), np.sin(fr)], 0)
    sc64 = np.concatenate([scale] * 2, 0)
    t["ics_full"] = np.tile(cos64 * sc64, (2, 1)).astype(np.float32)
    t["iss_full"] = np.tile(ssin64 * sc64, (2, 1)).astype(np.float32)
    t["ick_full"] = np.tile(cos64 / sc64, (2, 1)).astype(np.float32)
    t["isk_full"] = np.tile(ssin64 / sc64, (2, 1)).astype(np.float32)

    P = np.zeros((128, 128), dtype=np.float32)
    for blk in range(2):
        b0 = blk * 64
        for i in range(32):
            P[b0 + 32 + i, b0 + i] = 1.0
            P[b0 + i, b0 + 32 + i] = 1.0
    t["permB"] = P
    t["ident"] = np.eye(128, dtype=np.float32)
    t["identb"] = np.eye(128, dtype=np.float32)
    t["onesf"] = np.ones((128, 512), dtype=np.float32)
    t["onesb"] = np.ones((128, 512), dtype=np.float32)
    o2 = np.zeros((128, 2), dtype=np.float32)
    o2[0:64, 0] = 1.0
    o2[64:128, 1] = 1.0
    t["ones2"] = o2
    s2 = np.zeros((2, 128), dtype=np.float32)
    s2[0, 0:64] = 1.0
    s2[1, 64:128] = 1.0
    t["sel2"] = s2
    pr = np.zeros((128, QB), dtype=np.float32)
    pr[64, :] = np.arange(QB, dtype=np.float32)
    t["posrel64"] = pr
    m1 = np.where(np.arange(128)[None, :] <= np.arange(128)[:, None], 0.0, NEG)
    t["m1stair"] = m1.astype(np.float32)
    m2 = np.where(np.arange(128)[:, None] <= np.arange(128)[None, :], 0.0, NEG)
    t["m2stairT"] = m2.astype(np.float32)
    return t


def host_percore_tables(core, const):
    be, hf = core // 2, core % 2
    own0 = hf * NOWN
    t = {}
    t["ocos_q"] = np.ascontiguousarray(const["ocos"][:, own0:own0 + NOWN])
    t["osin_q"] = np.ascontiguousarray(const["osin"][:, own0:own0 + NOWN])
    for nm in ("ics", "iss", "ick", "isk"):
        t[nm] = np.ascontiguousarray(const[nm + "_full"][:, own0:own0 + NOWN])

    pk = np.zeros((NQB, 128, KT), dtype=np.float32)
    j = (np.arange(128)[:, None] + 128 * np.arange(KT)[None, :]).astype(np.float64)
    for qb in range(NQB):
        pk[qb] = (j - (own0 + qb * QB)).astype(np.float32)
    t["poskrel"] = pk

    mf = np.zeros((NQB, KT), dtype=np.float32)
    for qb in range(NQB):
        qend = own0 + qb * QB + QB - 1
        for kt in range(KT):
            if 128 * kt > qend:
                mf[qb, kt] = NEG
    t["omaskflag"] = mf

    oc = np.zeros((NQB, 8, 128, QB), dtype=np.float32)
    for qb in range(NQB):
        iabs = own0 + qb * QB + np.arange(QB)
        for s in range(8):
            kt = (4 * qb + s) if s < 4 else (16 + 4 * qb + (s - 4))
            if mf[qb, kt] == NEG:
                continue
            jabs = 128 * kt + np.arange(128)
            stair = np.where(jabs[:, None] <= iabs[None, :], 0.0, NEG)
            if np.any(stair == NEG):
                oc[qb, s] = stair
    t["ocausal"] = oc.astype(BF)

    w0m = np.zeros((128, 512), dtype=np.float32)
    if hf == 0:
        w0m[:, 0:256] = NEG
    t["w0prevmask"] = w0m

    q = np.arange(256)
    mm = np.zeros((2, 2, 128, 256), dtype=np.float32)
    for kt in range(2):
        jr = 128 * kt + np.arange(128)
        tooold = (jr[:, None] >= q[None, :]).astype(np.float32)
        mm[1, kt] = tooold
        mm[0, kt] = tooold if hf == 1 else 0.0
    t["mask01prev"] = mm.astype(BF)
    t["mask01neg"] = np.where(mm == 1.0, 0.0, NEG).astype(np.float32)

    t["padmask"] = np.full((128, 1), 0.0 if MB_SELF else float(hf),
                           dtype=np.float32)
    t["slot"] = np.array([[core & 1]], dtype=np.uint32)
    return t


def host_weight_layouts(inputs):
    w = {}
    wqkv = np.asarray(inputs["L0_attn_w_qkv"], dtype=np.float32)
    perm = np.concatenate([np.arange(0, DH, 2), np.arange(1, DH, 2)])
    wp = wqkv.copy()
    for part in range(2):
        for h in range(H):
            c0 = part * D + h * DH
            wp[:, c0:c0 + DH] = wqkv[:, c0 + perm]
    w["wqkv_o"] = np.ascontiguousarray(wp)
    w["wo_o"] = np.ascontiguousarray(np.asarray(inputs["L0_attn_w_o"], np.float32))
    lx = np.asarray(inputs["L0_attn_log_xi"], np.float32)
    w["logxi_r"] = np.ascontiguousarray(lx.reshape(1, H))
    et = -(np.arange(H) % 4 % 2).astype(np.float32)
    w["ebits_r"] = np.ascontiguousarray(et.reshape(1, H))

    w["wqkv_i"] = np.ascontiguousarray(np.asarray(inputs["L0_ffn_Wqkv"], np.float32))
    w["wo_i"] = np.ascontiguousarray(np.asarray(inputs["L0_ffn_Wo"], np.float32))
    qs = np.asarray(inputs["L0_ffn_q_scale"], np.float32)
    ks = np.asarray(inputs["L0_ffn_k_scale"], np.float32)
    w["qs2"] = np.ascontiguousarray(np.tile(qs, (1, 2)).reshape(DEPTH, 128, 1))
    w["ks2"] = np.ascontiguousarray(np.tile(ks, (1, 2)).reshape(DEPTH, 128, 1))
    w["lng"] = np.ascontiguousarray(np.asarray(inputs["L0_ffn_ln_g"], np.float32))
    w["lnb"] = np.ascontiguousarray(np.asarray(inputs["L0_ffn_ln_b"], np.float32))

    W1 = np.asarray(inputs["L0_ffn_W1"], np.float32)
    b1 = np.asarray(inputs["L0_ffn_b1"], np.float32)
    W2 = np.asarray(inputs["L0_ffn_W2"], np.float32)
    W1a = np.zeros((DEPTH, D, DIP), dtype=np.float32)
    W1g = np.zeros((DEPTH, D, DIP), dtype=np.float32)
    W1a[:, :, :DI] = W1[:, :, :DI]
    W1g[:, :, :DI] = W1[:, :, DI:]
    b1a = np.zeros((DEPTH, DIP), dtype=np.float32)
    b1g = np.zeros((DEPTH, DIP), dtype=np.float32)
    b1a[:, :DI] = b1[:, :DI]
    b1g[:, :DI] = b1[:, DI:]
    w["W1a"] = W1a
    w["W1g"] = W1g
    w["b1a"] = np.ascontiguousarray(b1a.reshape(DEPTH, NHT, 128).transpose(0, 2, 1))
    w["b1g"] = np.ascontiguousarray(b1g.reshape(DEPTH, NHT, 128).transpose(0, 2, 1))
    W2p = np.zeros((DEPTH, DIP, D), dtype=np.float32)
    W2p[:, :DI, :] = W2
    w["W2p"] = W2p
    w["b2"] = np.ascontiguousarray(np.asarray(inputs["L0_ffn_b2"], np.float32))
    for k in ("wqkv_o", "wo_o", "wqkv_i", "wo_i", "W1a", "W1g", "W2p", "b2"):
        w[k] = np.ascontiguousarray(w[k].astype(BF))
    return w



class P:
    DMA_OWNER = {"sq": "sync", "gq": "gpsimd"}

    def __init__(self, nc):
        self.nc = nc
        self.q = {e: [] for e in ENGS}
        self.sems = {}
        self.cnt = {}
        self.blocks = 0
        self.dmat = {"sq": 0, "gq": 0}
        self.dmsw = {"sq": 0, "gq": 0}

    def sem(self, name):
        h = self.nc.alloc_semaphore(name)
        self.sems[name] = h
        self.cnt[name] = 0
        return h

    def _wait(self, eng, waits):
        for (s, v) in waits:
            if v <= 0:
                continue
            assert v <= self.cnt[s], f"wait {s}>={v} emitted={self.cnt[s]}"
            if s in self.dmat:
                if eng == self.DMA_OWNER[s]:
                    self.dmsw[s] = max(self.dmsw[s], v)
                else:
                    self.dmat[s] = max(self.dmat[s], v)
            h = self.sems[s]
            self.q[eng].append(lambda e, h=h, v=v: e.wait_ge(h, v))

    def dma_pin(self, s):
        if self.dmat[s] > self.dmsw[s]:
            v = self.dmat[s]
            h = self.sems[s]
            self.q[self.DMA_OWNER[s]].append(
                lambda e, h=h, v=v: e.wait_ge(h, v))
            self.dmsw[s] = v

    def do(self, eng, fn, waits=(), inc=None, incby=1):
        self._wait(eng, waits)
        if inc in self.dmat:
            self.dma_pin(inc)
        if inc is not None:
            self.cnt[inc] += incby
            h = self.sems[inc]
            if incby == 1 and eng in ("vector", "scalar", "gpsimd"):
                def _f(e, fn=fn, h=h, n=incby):
                    fn(e)
                    e.drain().then_inc(h, n)
                self.q[eng].append(_f)
            else:
                self.q[eng].append(
                    lambda e, fn=fn, h=h, n=incby: fn(e).then_inc(h, n))
        else:
            self.q[eng].append(fn)
        return self.cnt[inc] if inc is not None else None

    def raw(self, eng, fn):
        self.q[eng].append(fn)

    def val(self, name):
        return self.cnt[name]

    def bar(self):
        self.do("sync", lambda x: x.nop(), inc="ye")
        for e, s in (("tensor", "te"), ("vector", "ve"), ("scalar", "se"),
                     ("gpsimd", "ge")):
            h = self.sems[s]
            self.cnt[s] += 1
            self.q[e].append(lambda x, h=h: x.drain().then_inc(h, 1))
        vals = [(s, self.cnt[s]) for s in
                ("te", "ve", "se", "ge", "ye", "sq", "gq")]
        for e in ENGS:
            self._wait(e, vals)

    def flush(self):
        nq = {e: ops for e, ops in self.q.items() if ops}
        self.q = {e: [] for e in ENGS}
        if not nq:
            return
        self.blocks += 1
        with self.nc.Block(f"blk{self.blocks}") as blk:
            for e, ops in nq.items():
                getattr(blk, e)(lambda eng, ops=ops: [f(eng) for f in ops])



def build_program(debug=()):
    nc = bass.Bass("TRN2", target_bir_lowering=False)

    def din(name, shape, dt=F32):
        return nc.dram_tensor(name, list(shape), dt, kind="ExternalInput").ap()

    xfull = din("xfull", (N, D))
    xown = din("xown", (NOWN, D))
    wqkv_o = din("wqkv_o", (D, 3 * D), BF16)
    wo_o = din("wo_o", (D, D), BF16)
    logxi_r = din("logxi_r", (1, H))
    ebits_r = din("ebits_r", (1, H))
    ocos = din("ocos", (128, N))
    osin = din("osin", (128, N))
    ocos_q = din("ocos_q", (128, NOWN))
    osin_q = din("osin_q", (128, NOWN))
    ics = din("ics", (128, NOWN))
    iss = din("iss", (128, NOWN))
    ick = din("ick", (128, NOWN))
    isk = din("isk", (128, NOWN))
    permB = din("permB", (128, 128))
    ident = din("ident", (128, 128))
    identb = din("identb", (128, 128))
    onesf = din("onesf", (128, 512))
    onesb = din("onesb", (128, 512))
    ones2 = din("ones2", (128, 2))
    sel2 = din("sel2", (2, 128))
    posrel64 = din("posrel64", (128, QB))
    m1stair = din("m1stair", (128, 128))
    m2stairT = din("m2stairT", (128, 128))
    poskrel = din("poskrel", (NQB, 128, KT))
    omaskflag = din("omaskflag", (NQB, KT))
    ocausal = din("ocausal", (NQB, 8, 128, QB), BF16)
    w0prevmask = din("w0prevmask", (128, 512))
    mask01prev = din("mask01prev", (2, 2, 128, 256), BF16)
    mask01neg = din("mask01neg", (2, 2, 128, 256))
    padmask = din("padmask", (128, 1))
    slot_in = din("slot", (1, 1), U32)
    fvals = din("fvals", (1, 64), U32)
    wqkv_i = din("wqkv_i", (DEPTH, D, 3 * D), BF16)
    wo_i = din("wo_i", (DEPTH, D, D), BF16)
    qs2 = din("qs2", (DEPTH, 128, 1))
    ks2 = din("ks2", (DEPTH, 128, 1))
    lng = din("lng", (DEPTH, D))
    lnb = din("lnb", (DEPTH, D))
    W1a = din("W1a", (DEPTH, D, DIP), BF16)
    W1g = din("W1g", (DEPTH, D, DIP), BF16)
    b1a_i = din("b1a", (DEPTH, 128, NHT))
    b1g_i = din("b1g", (DEPTH, 128, NHT))
    W2p = din("W2p", (DEPTH, DIP, D), BF16)
    b2_i = din("b2", (DEPTH, D), BF16)

    y = nc.dram_tensor("y", [NOWN, D], F32, kind="ExternalOutput").ap()
    dbg = {}
    for nm, shape in debug:
        dbg[nm] = nc.dram_tensor("d_" + nm, list(shape), F32,
                                 kind="ExternalOutput").ap()

    kscr = nc.dram_tensor("kscr", [8, 128, N], BF16, kind="Internal").ap()
    qscr = nc.dram_tensor("qscr", [8, 128, NOWN], BF16, kind="Internal").ap()
    vscr_o = nc.dram_tensor("vscr_o", [KT, 128, HG], BF16, kind="Internal").ap()
    x1scr = nc.dram_tensor("x1scr", [8, 128, NOWN], F32, kind="Internal").ap()
    vscr_i = nc.dram_tensor("vscr_i", [2 + NOWN // 128, 128, HG], BF16,
                            kind="Internal").ap()
    kscr_i = nc.dram_tensor("kscr_i", [8, 128, NOWN], BF16,
                            kind="Internal").ap()
    qscr_i = nc.dram_tensor("qscr_i", [8, 128, NOWN], BF16,
                            kind="Internal").ap()
    mbsp = {} if MB_SELF else {"addr_space": "Shared"}
    mb_x = nc.dram_tensor("mb_x", [2, 2, 128, 8], F32, kind="Internal",
                          **mbsp).ap()
    mb_k = nc.dram_tensor("mb_k", [2, 2, 8, 128, WIN], BF16, kind="Internal",
                          **mbsp).ap()
    mb_v = nc.dram_tensor("mb_v", [2, 2, 2, 128, HG], BF16, kind="Internal",
                          **mbsp).ap()
    mb_fl = nc.dram_tensor("mb_fl", [2, 64], U32, kind="Internal",
                           **mbsp).ap()

    p = P(nc)
    for s in ("sq", "gq", "te", "ve", "se", "ge", "ye", "pl"):
        p.sem(s)

    st = contextlib.ExitStack()
    sb = lambda name, shape, dt: st.enter_context(
        nc.sbuf_tensor(name, list(shape), dt))
    PS = [st.enter_context(nc.psum_tensor(f"ps{i}", [128, 512], F32))
          for i in range(8)]

    xres = [sb(f"xres{i}", (128, 1 + NOWN), F32) for i in range(8)]
    t_onesf = sb("c_onesf", (128, 512), F32)
    t_onesb = sb("c_onesb", (128, 512), BF16)
    t_ident = sb("c_ident", (128, 128), F32)
    t_identb = sb("c_identb", (128, 128), BF16)
    t_perm = sb("c_perm", (128, 128), BF16)
    t_ones2 = sb("c_ones2", (128, 2), BF16)
    t_sel2 = sb("c_sel2", (2, 128), F32)
    t_pad = sb("c_pad", (128, 1), F32)
    t_slot = sb("c_slot", (1, 1), U32)
    t_fv = sb("c_fv", (1, 64), U32)
    t_stage = sb("c_stage", (128, 1024), F32)
    t_stageb = sb("c_stageb", (128, 1024), BF16)
    t_m1 = sb("c_m1", (128, 128), F32)
    t_m2T = sb("c_m2T", (128, 128), F32)
    t_w0pm = sb("c_w0pm", (128, 512), F32)
    t_m01n = sb("c_m01n", (128, 2 * 2 * 256), F32)
    t_flagr = sb("c_flagr", (1, 64), U32)
    t_xcol = sb("c_xcol", (128, 8), F32)
    t_xcolr = sb("c_xcolr", (128, 8), F32)
    t_cast = sb("t_cast", (128, 512), BF16)
    t_cast2 = sb("t_cast2", (128, 512), BF16)
    t_mu = sb("t_mu", (1, 512), F32)
    t_rstd = sb("t_rstd", (1, 512), F32)
    t_tmp1 = sb("t_tmp1", (1, 512), F32)
    st2 = contextlib.ExitStack()
    sb2 = lambda name, shape, dt: st2.enter_context(
        nc.sbuf_tensor(name, list(shape), dt))
    t_c8 = sb2("c_c8", (128, H), F32)
    t_negc = sb2("c_negc", (128, H), F32)
    t_btab = sb2("c_btab", (128, NQB * H * KT), F32)
    t_posr64 = sb2("c_posr64", (128, QB), F32)

    for dst, src in ((t_onesf[:], onesf), (t_ident[:], ident),
                     (t_m1[:], m1stair), (t_m2T[:], m2stairT),
                     (t_w0pm[:], w0prevmask), (t_pad[:], padmask),
                     (t_slot[:], slot_in), (t_fv[:], fvals),
                     (t_sel2[:], sel2), (t_posr64[:], posrel64)):
        p.do("sync", lambda e, d=dst, s=src: e.dma_start(out=d, in_=s),
             inc="sq", incby=16)
    for dst, src in ((t_onesb[:], onesb), (t_perm[:], permB),
                     (t_ones2[:], ones2), (t_identb[:], identb)):
        p.do("gpsimd", lambda e, d=dst, s=src: e.dma_start(out=d, in_=s),
             inc="gq", incby=16)
    p.do("gpsimd", lambda e: e.dma_start(
        out=t_m01n.rearrange("p (a b f) -> p a b f", a=2, b=2),
        in_=mask01neg.rearrange("a b p f -> p a b f")), inc="gq", incby=16)
    p.bar()

    mbx = {"n": 0}

    def _mb_init(e):
        mbx["myr"] = e.alloc_register("myslotr")
        e.reg_load(mbx["myr"], t_slot[0:1, 0:1])
        mbx["tgt"] = e.alloc_register("polltgt")
        e.reg_mov(mbx["tgt"], 0)
    if STOP_AFTER != "consts_nomb":
        p.raw("sync", _mb_init)
    if STOP_AFTER in ("consts", "consts_nomb"):
        yv0 = y.rearrange("t d -> (t d)").rearrange("(a c) -> a c", a=1024)
        for pt in range(8):
            p.do("sync", lambda e, pt=pt: e.dma_start(
                out=yv0[pt * 128:(pt + 1) * 128, :],
                in_=xres[pt][:, 1:1 + NOWN]), inc="sq", incby=16)
        p.bar()
        p.flush()
        st2.close()
        st.close()
        return nc

    def mb_dual(ap0_fn, ap1_fn, inc=None, incby=16, waits=(), eng="sync"):
        p._wait(eng, waits)
        if inc in p.dmat:
            p.dma_pin(inc)
        if MB_SELF:
            if inc is not None:
                base = p.cnt[inc]
                p.cnt[inc] += 2 * incby
                h = p.sems[inc]

                def _f2(e, a0=ap0_fn, a1=ap1_fn, h=h, n=incby, base=base):
                    a0(e).then_inc(h, n)
                    e.wait_ge(h, base + n)
                    a1(e).then_inc(h, n)
                p.q[eng].append(_f2)
                return p.cnt[inc]

            def _g2(e, a0=ap0_fn, a1=ap1_fn):
                a0(e)
                a1(e)
            p.q[eng].append(_g2)
            return None
        if inc is not None:
            p.cnt[inc] += incby
            h = p.sems[inc]
            def _f(e, a0=ap0_fn, a1=ap1_fn, h=h, n=incby):
                with e.If_eq(mbx["myr"], 0):
                    a0(e).then_inc(h, n)
                with e.Else():
                    a1(e).then_inc(h, n)
            p.q[eng].append(_f)
            return p.cnt[inc]
        def _g(e, a0=ap0_fn, a1=ap1_fn):
            with e.If_eq(mbx["myr"], 0):
                a0(e)
            with e.Else():
                a1(e)
        p.q[eng].append(_g)


    def dprobe(tag):
        pass

    def mb_flag(waits=()):
        k = mbx["n"]
        mbx["n"] += 1
        assert k < 64
        mb_dual(lambda e, k=k: e.dma_start(out=mb_fl[0, k:k + 1],
                                           in_=t_fv[0:1, k:k + 1]),
                lambda e, k=k: e.dma_start(out=mb_fl[1, k:k + 1],
                                           in_=t_fv[0:1, k:k + 1]),
                inc="sq", waits=waits)
        return k

    def mb_poll(k):
        sqv = p.val("sq")

        def _poll(e):
            if MB_SELF:
                e.wait_ge(p.sems["sq"], sqv)
            with e.register(f"c{k}") as cond, e.register(f"f{k}") as fvr, \
                 e.register(f"x{k}") as expr:
                e.reg_load(expr, t_fv[0:1, k:k + 1])
                e.reg_mov(cond, 1)
                with e.While(cond):
                    e.reg_alu(mbx["tgt"], mbx["tgt"], 16, ALU.add)
                    with e.If_eq(mbx["myr"], 0):
                        e.dma_start(out=t_flagr[0:1, k:k + 1],
                                    in_=mb_fl[1, k:k + 1]
                                    ).then_inc(p.sems["pl"], 16)
                    with e.Else():
                        e.dma_start(out=t_flagr[0:1, k:k + 1],
                                    in_=mb_fl[0, k:k + 1]
                                    ).then_inc(p.sems["pl"], 16)
                    e.wait_ge(p.sems["pl"], mbx["tgt"])
                    e.reg_load(fvr, t_flagr[0:1, k:k + 1])
                    e.reg_alu(cond, fvr, expr, ALU.not_equal)
        p.raw("sync", _poll)

    def dbg_dump(name, pairs):
        if name not in dbg:
            return
        for sap, dap in pairs:
            p.do("sync", lambda e, s=sap, d=dap: e.dma_start(out=d, in_=s),
                 inc="sq", incby=16)
        p.bar()

    class _SkipRest(Exception):
        pass

    def dump_stop_early():
        yv0 = y.rearrange("t d -> (t d)").rearrange("(a c) -> a c", a=1024)
        p.bar()
        if not SIM_SKIP_UNINIT_DUMP:
            for pt in range(8):
                p.do("sync", lambda e, pt=pt: e.dma_start(
                    out=yv0[pt * 128:(pt + 1) * 128, :],
                    in_=xres[pt][:, 1:1 + NOWN]), inc="sq", incby=16)
            p.bar()
        p.flush()
        st2.close()
        st.close()
        return nc

    with nc.sbuf_tensor("t_lxr", [1, H], F32) as t_lxr, \
         nc.sbuf_tensor("t_ebr", [1, H], F32) as t_ebr, \
         nc.sbuf_tensor("t_crw", [1, H], F32) as t_crw, \
         nc.sbuf_tensor("t_pk", [128, NQB * KT], F32) as t_pk, \
         nc.sbuf_tensor("t_omf", [1, NQB * KT], F32) as t_omf, \
         nc.sbuf_tensor("t_omfb", [128, NQB * KT], F32) as t_omfb:
        for d_, s_ in ((t_lxr[:], logxi_r), (t_ebr[:], ebits_r),
                       (t_omf[:], omaskflag.rearrange("a b -> (a b)").rearrange("(o c) -> o c", o=1))):
            p.do("sync", lambda e, d=d_, s=s_: e.dma_start(out=d, in_=s),
                 inc="sq", incby=16)
        p.do("sync", lambda e: e.dma_start(
            out=t_pk.rearrange("p (a b) -> p a b", a=NQB),
            in_=poskrel.rearrange("a p b -> p a b")), inc="sq", incby=16)
        p.bar()
        _s = STOP_AFTER or ""
        if _s.startswith("p0b4"):
            stoplv = 4
        elif _s in ("p0b1", "p0b2", "p0b3"):
            stoplv = int(_s[-1])
        else:
            stoplv = {"p0a": 0, "p0b": 5, "p0c": 6}.get(_s, 9)
        if stoplv >= 1:
            p.do("scalar", lambda e: e.activation(out=t_crw[:], in_=t_lxr[:],
                                                  func=AF.Exp, scale=-1.0),
                 inc="se")
            if stoplv >= 2:
                p.do("vector", lambda e: e.tensor_mul(t_crw[:], t_crw[:],
                                                      t_ebr[:]),
                     waits=(("se", p.val("se")),), inc="ve")
            if stoplv >= 3:
                p.do("tensor", lambda e: e.matmul(PS[0][:, 0:H],
                                                  t_onesf[0:1, 0:128],
                                                  t_crw[:], start=True,
                                                  stop=True),
                     waits=(("ve", p.val("ve")),), inc="te")
            if stoplv >= 4:
                b4v = _s[4:] if _s.startswith("p0b4") else ""
                if b4v == "a":
                    p.do("scalar", lambda e: e.mul(t_c8[:], t_onesf[:, 0:H],
                                                   8.0),
                         waits=(("te", p.val("te")),), inc="se")
                elif b4v == "b":
                    p._wait("scalar", (("te", p.val("te")),))
                    p.cnt["se"] += 1
                    p.raw("scalar", lambda e: e.mul(
                        t_c8[:], PS[0][:, 0:H], 8.0).then_inc(
                            p.sems["se"], 1))
                elif b4v == "c":
                    p.do("vector", lambda e: e.tensor_scalar(
                        out=t_c8[:], in0=PS[0][:, 0:H], scalar1=8.0,
                        scalar2=None, op0=ALU.mult),
                        waits=(("te", p.val("te")),), inc="ve")
                elif b4v == "e":
                    p.do("scalar", lambda e: e.activation(
                        out=t_c8[:], in_=PS[0][:, 0:H], func=AF.Copy,
                        scale=8.0),
                        waits=(("te", p.val("te")),), inc="se")
                else:
                    p.do("scalar", lambda e: e.mul(t_c8[:], PS[0][:, 0:H],
                                                   8.0),
                         waits=(("te", p.val("te")),), inc="se")
            if stoplv >= 5:
                p.do("scalar", lambda e: e.mul(t_negc[:], PS[0][:, 0:H],
                                               -1.0), inc="se")
            p.bar()
        if stoplv >= 6:
            for qb in range(NQB):
                p.do("tensor", lambda e, qb=qb: e.matmul(
                    PS[1][:, qb * KT:(qb + 1) * KT], t_onesf[0:1, 0:128],
                    t_omf[0:1, qb * KT:(qb + 1) * KT],
                    start=(qb == 0), stop=(qb == NQB - 1)),
                    inc="te" if qb == NQB - 1 else None)
            p.do("scalar", lambda e: e.copy(t_omfb[:], PS[1][:, 0:NQB * KT]),
                 waits=(("te", p.val("te")),), inc="se")
        if stoplv >= 7:
            s1 = p.val("se")
            for qb in range(NQB):
                for h in range(H):
                    c0 = ((qb * H) + h) * KT
                    p.do("vector", lambda e, qb=qb, h=h, c0=c0:
                         e.scalar_tensor_tensor(
                             out=t_btab[:, c0:c0 + KT],
                             in0=t_pk[:, qb * KT:(qb + 1) * KT],
                             scalar=t_negc[:, h:h + 1],
                             in1=t_omfb[:, qb * KT:(qb + 1) * KT],
                             op0=ALU.mult, op1=ALU.add),
                         waits=(("se", s1),) if (qb == 0 and h == 0) else ())
        p.bar()
        p.flush()
    if stoplv <= 6:
        return dump_stop_early()

    if STOP_AFTER == "p0":
        yv0 = y.rearrange("t d -> (t d)").rearrange("(a c) -> a c", a=1024)
        p.bar()
        if not SIM_SKIP_UNINIT_DUMP:
            for pt in range(8):
                p.do("sync", lambda e, pt=pt: e.dma_start(
                    out=yv0[pt * 128:(pt + 1) * 128, :],
                    in_=xres[pt][:, 1:1 + NOWN]), inc="sq", incby=16)
            p.bar()
        p.flush()
        st2.close()
        st.close()
        return nc

    def emit_ln_stats(srcs):
        for pt in range(8):
            sl = srcs[pt]
            w = [("te", p.val("te"))] if pt > 0 else ()
            p.do("scalar", lambda e, sl=sl: e.copy(t_cast[:], sl),
                 waits=w, inc="se")
            p.do("scalar", lambda e, sl=sl: e.activation(
                out=t_cast2[:], in_=sl, func=AF.Square), inc="se")
            sv = p.val("se")
            p.do("tensor", lambda e, pt=pt: e.matmul(
                PS[2][0:1, 0:512], t_onesb[:, 0:1], t_cast[:],
                start=(pt == 0), stop=(pt == 7)), waits=(("se", sv),))
            p.do("tensor", lambda e, pt=pt: e.matmul(
                PS[3][0:1, 0:512], t_onesb[:, 0:1], t_cast2[:],
                start=(pt == 0), stop=(pt == 7)), inc="te")
        tw = p.val("te")
        p.do("scalar", lambda e: e.mul(t_mu[:], PS[2][0:1, 0:512], 1.0 / D),
             waits=(("te", tw),), inc="se")
        p.do("scalar", lambda e: e.mul(t_tmp1[:], PS[3][0:1, 0:512], 1.0 / D),
             inc="se")
        sv = p.val("se")
        p.do("vector", lambda e: e.tensor_mul(t_rstd[:], t_mu[:], t_mu[:]),
             waits=(("se", sv),), inc="ve")
        p.do("vector", lambda e: e.scalar_tensor_tensor(
            out=t_tmp1[:], in0=t_tmp1[:], scalar=1e-5, in1=t_rstd[:],
            op0=ALU.add, op1=ALU.subtract),
            waits=(("ve", p.val("ve")),), inc="ve")
        p.do("scalar", lambda e: e.activation(out=t_tmp1[:], in_=t_tmp1[:],
                                              func=AF.Sqrt),
             waits=(("ve", p.val("ve")),), inc="se")
        p.do("vector", lambda e: e.reciprocal(t_rstd[:], t_tmp1[:]),
             waits=(("se", p.val("se")),), inc="ve")
        p.do("vector", lambda e: e.scalar_tensor_tensor(
            out=t_mu[:], in0=t_mu[:], scalar=-1.0, in1=t_rstd[:],
            op0=ALU.mult, op1=ALU.mult),
            waits=(("ve", p.val("ve")),), inc="ve")
        p.bar()

    dprobe('postP0')
    with nc.sbuf_tensor("wqb", [128, 8 * 3072], BF16) as wqb, \
         nc.sbuf_tensor("xtm", [128, 4 * 1024], F32) as xtm, \
         nc.sbuf_tensor("xbb", [128, 8 * 512], BF16) as xbb, \
         nc.sbuf_tensor("tcs", [128, 512], F32) as tcs, \
         nc.sbuf_tensor("tsn", [128, 512], F32) as tsn, \
         nc.sbuf_tensor("tf_", [128, 512], F32) as tf_, \
         nc.sbuf_tensor("tb_", [128, 512], BF16) as tb_, \
         nc.sbuf_tensor("tm1_", [128, 512], F32) as tm1_, \
         nc.sbuf_tensor("tm2_", [128, 512], F32) as tm2_, \
         nc.sbuf_tensor("kout", [128, 512], BF16) as kout, \
         nc.sbuf_tensor("vblk", [128, HG], BF16) as vblk:

        for kt in range(8):
            p.do("gpsimd", lambda e, kt=kt: e.dma_start(
                out=wqb[:, kt * 3072:(kt + 1) * 3072],
                in_=wqkv_o[kt * 128:(kt + 1) * 128, :]), inc="gq", incby=16)
        p.bar()

        def xpose_block(src, blk, fill_xres):
            for tt in range(4):
                p.do("sync", lambda e, tt=tt, blk=blk, src=src: e.dma_start(
                    out=xtm[:, tt * 1024:(tt + 1) * 1024],
                    in_=src[blk * 512 + tt * 128: blk * 512 + (tt + 1) * 128, :]),
                    inc="sq", incby=16)
            lw = p.val("sq")
            for i, (tt, pt) in enumerate(
                    (t, q) for t in range(4) for q in range(8)):
                pb = PS[i % 2]
                w = [("sq", lw)] if i == 0 else [("se", p.val("se") - 1)]
                p.do("tensor", lambda e, tt=tt, pt=pt, pb=pb: e.transpose(
                    pb[:, 0:128],
                    xtm[:, tt * 1024 + pt * 128: tt * 1024 + (pt + 1) * 128],
                    t_ident[:]), waits=w, inc="te")
                tw = p.val("te")
                p.do("scalar", lambda e, tt=tt, pt=pt, pb=pb: e.copy(
                    xbb[:, pt * 512 + tt * 128: pt * 512 + (tt + 1) * 128],
                    pb[:, 0:128]), waits=(("te", tw),), inc="se")
                if fill_xres:
                    p.do("scalar",
                         lambda e, tt=tt, pt=pt, pb=pb, blk=blk: e.copy(
                             xres[pt][:, 1 + blk * 512 + tt * 128:
                                      1 + blk * 512 + (tt + 1) * 128],
                             pb[:, 0:128]), inc="se")
            p.bar()

        def rope_pair(psum, dst):
            p.do("scalar", lambda e, psum=psum: e.copy(tf_[:], psum[:, 0:512]),
                 inc="se")
            p.do("scalar", lambda e, psum=psum: e.copy(tb_[:], psum[:, 0:512]),
                 inc="se")
            sw = p.val("se")
            p.do("tensor", lambda e: e.matmul(PS[6][:, 0:512], t_perm[:],
                                              tb_[:], start=True, stop=True),
                 waits=(("se", sw),), inc="te")
            tpm = p.val("te")
            p.do("vector", lambda e: e.tensor_mul(tm1_[:], tf_[:], tcs[:]),
                 waits=(("se", sw),))
            p.do("vector", lambda e: e.tensor_mul(tm2_[:], tsn[:],
                                                  PS[6][:, 0:512]),
                 waits=(("te", tpm),), inc="ve")
            p.do("vector", lambda e, dst=dst: e.tensor_add(dst, tm1_[:],
                                                           tm2_[:]),
                 waits=(("ve", p.val("ve")),), inc="ve")

        def qk_pass(src_x, wcol0, cost, sint, dst_scr, fill_xres, nblk):
            for blk in range(nblk):
                xpose_block(src_x, blk, fill_xres)
                p.do("sync", lambda e, blk=blk, cost=cost: e.dma_start(
                    out=tcs[:], in_=cost[:, blk * 512:(blk + 1) * 512]),
                    inc="sq", incby=16)
                p.do("sync", lambda e, blk=blk, sint=sint: e.dma_start(
                    out=tsn[:], in_=sint[:, blk * 512:(blk + 1) * 512]),
                    inc="sq", incby=16)
                p.bar()
                for m in range(8):
                    pk = PS[2 + (m % 2)]
                    for kt in range(8):
                        p.do("tensor", lambda e, m=m, kt=kt, pk=pk,
                             wcol0=wcol0: e.matmul(
                                 pk[:, 0:512],
                                 wqb[:, kt * 3072 + wcol0 + m * 128:
                                     kt * 3072 + wcol0 + (m + 1) * 128],
                                 xbb[:, kt * 512:(kt + 1) * 512],
                                 start=(kt == 0), stop=(kt == 7)),
                             inc="te" if kt == 7 else None)
                    tmm = p.val("te")
                    if m % 2 == 1:
                        p._wait("scalar", (("te", tmm),))
                        rope_pair(pk, kout[:])
                        vv = p.val("ve")
                        p.do("sync", lambda e, m=m, blk=blk,
                             dst_scr=dst_scr: e.dma_start(
                                 out=dst_scr[m, :, blk * 512:(blk + 1) * 512],
                                 in_=kout[:]),
                             waits=(("ve", vv),), inc="sq", incby=16)
                    else:
                        p.do("scalar", lambda e, pk=pk: e.copy(
                            kout[:], pk[:, 0:512]),
                            waits=(("te", tmm),), inc="se")
                        sv = p.val("se")
                        p.do("sync", lambda e, m=m, blk=blk,
                             dst_scr=dst_scr: e.dma_start(
                                 out=dst_scr[m, :, blk * 512:(blk + 1) * 512],
                                 in_=kout[:]),
                             waits=(("se", sv),), inc="sq", incby=16)
                    p.bar()
                yield blk

        for blk in qk_pass(xfull, 1024, ocos, osin, kscr, False, 8):
            for tt in range(4):
                for kt in range(8):
                    p.do("tensor", lambda e, tt=tt, kt=kt: e.matmul(
                        PS[4][:, 0:512],
                        xbb[:, kt * 512 + tt * 128: kt * 512 + (tt + 1) * 128],
                        wqb[:, kt * 3072 + 2048: kt * 3072 + 2560],
                        start=(kt == 0), stop=(kt == 7)))
                    p.do("tensor", lambda e, tt=tt, kt=kt: e.matmul(
                        PS[5][:, 0:512],
                        xbb[:, kt * 512 + tt * 128: kt * 512 + (tt + 1) * 128],
                        wqb[:, kt * 3072 + 2560: kt * 3072 + 3072],
                        start=(kt == 0), stop=(kt == 7)),
                        inc="te" if kt == 7 else None)
                tv = p.val("te")
                p.do("vector", lambda e: e.memset(
                    vblk.rearrange("p (h c) -> p h c", h=H)[:, :, 64:65], 1.0),
                    waits=(("te", tv),), inc="ve")
                vv = p.val("ve")
                for half, pv in ((0, PS[4]), (1, PS[5])):
                    p.do("scalar", lambda e, half=half, pv=pv: e.copy(
                        vblk.rearrange("p (h c) -> p h c", h=H)[
                            :, half * 8:(half + 1) * 8, 0:64],
                        pv[:, 0:512].rearrange("p (h c) -> p h c", h=8)),
                        waits=(("ve", vv),), inc="se")
                sv = p.val("se")
                p.do("sync", lambda e, blk=blk, tt=tt: e.dma_start(
                    out=vscr_o[blk * 4 + tt], in_=vblk[:]),
                    waits=(("se", sv),), inc="sq", incby=16)
                p.bar()
        for blk in qk_pass(xown, 0, ocos_q, osin_q, qscr, True, 4):
            pass
        p.flush()

    dbg_dump("x0res", [(xres[pt][:, 1:1 + NOWN],
                        dbg["x0res"][pt * 128:(pt + 1) * 128, :])
                       for pt in range(8)] if "x0res" in dbg else [])

    dprobe('postP1')
    yv = y.rearrange("t d -> (t d)").rearrange("(a c) -> a c", a=1024)

    def dump_xres_and_stop():
        p.bar()
        for pt in range(8):
            p.do("sync", lambda e, pt=pt: e.dma_start(
                out=yv[pt * 128:(pt + 1) * 128, :],
                in_=xres[pt][:, 1:1 + NOWN]), inc="sq", incby=16)
        p.bar()
        p.flush()
        try:
            st2.close()
        except Exception:
            pass
        st.close()
        return nc

    if STOP_AFTER == "p1":
        return dump_xres_and_stop()

    with nc.sbuf_tensor("t_qqb", [128, 8 * 512], BF16) as t_qqb, \
         nc.sbuf_tensor("t_kp", [128, KT * 128], BF16) as t_kp, \
         nc.sbuf_tensor("t_vp", [128, KT * 130], BF16) as t_vp, \
         nc.sbuf_tensor("t_oc", [128, 8 * 512], BF16) as t_oc, \
         nc.sbuf_tensor("t_pt0", [128, 512], BF16) as t_pt0, \
         nc.sbuf_tensor("t_pt1", [128, 512], BF16) as t_pt1, \
         nc.sbuf_tensor("t_rec", [128, 512], F32) as t_rec, \
         nc.sbuf_tensor("t_obuf", [128, 8 * 512], BF16) as t_obuf, \
         nc.sbuf_tensor("t_wob", [128, 8 * 1024], BF16) as t_wob, \
         nc.sbuf_tensor("t_ln", [128, 8 * 512], F32) as t_ln:

        t_pts = (t_pt0, t_pt1)
        for kt in range(8):
            p.do("gpsimd", lambda e, kt=kt: e.dma_start(
                out=t_wob[:, kt * 1024:(kt + 1) * 1024],
                in_=wo_o[kt * 128:(kt + 1) * 128, :]), inc="gq", incby=16)
        p.bar()
        for qb in range(NQB):
            TB = 20 + 4 * qb
            stair_slots = {(4 * qb + m): m for m in range(4)}
            stair_slots.update({(16 + 4 * qb + m): 4 + m for m in range(4)})
            p.do("sync", lambda e, qb=qb: e.dma_start(
                out=t_qqb.rearrange("p (a c) -> p a c", a=8),
                in_=qscr[:, :, qb * QB:(qb + 1) * QB].rearrange(
                    "a p c -> p a c")), inc="sq", incby=16)
            p.do("sync", lambda e, qb=qb: e.dma_start(
                out=t_oc.rearrange("p (a c) -> p a c", a=8),
                in_=ocausal[qb].rearrange("a p c -> p a c")),
                inc="sq", incby=16)
            p.bar()
            for hp in range(8):
                p.do("sync", lambda e, hp=hp: e.dma_start(
                    out=t_kp[:], in_=kscr[hp]), inc="sq", incby=16)
                p.do("sync", lambda e, hp=hp: e.dma_start(
                    out=t_vp.rearrange("p (a c) -> p a c", a=KT, c=130),
                    in_=vscr_o[:, :, hp * 130:(hp + 1) * 130].rearrange(
                        "a p c -> p a c")), inc="sq", incby=16)
                p.bar()
                for hh in range(2):
                    h = 2 * hp + hh
                    p.do("vector", lambda e, h=h, hh=hh: e.tensor_scalar(
                        out=t_stage[64:65, hh * 512:(hh + 1) * 512],
                        in0=t_posr64[64:65, 0:QB],
                        scalar1=t_c8[64:65, h:h + 1], scalar2=None,
                        op0=ALU.mult), inc="ve")
                    stg = p.val("ve")
                    p._wait("tensor", (("ve", stg),))
                    expdone, avdone = {}, {}
                    for kt in range(TB):
                        psS = PS[kt % 2]
                        w = [("se", expdone[kt - 2])] if kt >= 2 else []
                        p.do("tensor", lambda e, kt=kt, hh=hh, hp=hp, psS=psS:
                             e.matmul(psS[:, 0:512],
                                      t_kp[hh * 64:(hh + 1) * 64,
                                           kt * 128:(kt + 1) * 128],
                                      t_qqb[hh * 64:(hh + 1) * 64,
                                            hp * 512:(hp + 1) * 512],
                                      start=True, stop=False), waits=w)
                        p.do("tensor", lambda e, hh=hh, psS=psS: e.matmul(
                            psS[:, 0:512], t_onesf[64:65, 0:128],
                            t_stage[64:65, hh * 512:(hh + 1) * 512],
                            start=False, stop=True), inc="te")
                        tmm = p.val("te")
                        if kt in stair_slots:
                            s = stair_slots[kt]
                            p.do("vector", lambda e, s=s, psS=psS: e.tensor_add(
                                psS[:, 0:512], psS[:, 0:512],
                                t_oc[:, s * 512:(s + 1) * 512]),
                                waits=(("te", tmm),), inc="ve")
                            mw = [("ve", p.val("ve"))]
                        else:
                            mw = [("te", tmm)]
                        if kt >= 2:
                            mw.append(("te", avdone[kt - 2]))
                        bcol = ((qb * H) + h) * KT + kt
                        p.do("scalar", lambda e, kt=kt, bcol=bcol, psS=psS:
                             e.activation(out=t_pts[kt % 2][:],
                                          in_=psS[:, 0:512], func=AF.Exp,
                                          scale=0.125,
                                          bias=t_btab[:, bcol:bcol + 1]),
                             waits=mw, inc="se")
                        expdone[kt] = p.val("se")
                        p.do("tensor", lambda e, kt=kt, hh=hh, TB=TB: e.matmul(
                            PS[4][0:65, 0:512],
                            t_vp[:, kt * 130 + hh * 65:
                                 kt * 130 + (hh + 1) * 65],
                            t_pts[kt % 2][:],
                            start=(kt == 0), stop=(kt == TB - 1)),
                            waits=(("se", expdone[kt]),), inc="te")
                        avdone[kt] = p.val("te")
                    tw = p.val("te")

                    def _recip_o(e):
                        with nc.allow_low_precision(
                                reason="recip feeds bf16 matmul; same "
                                       "precision as prior copy-cast"):
                            e.reciprocal(t_stageb[64:65, 0:512],
                                         PS[4][64:65, 0:512])
                    p.do("vector", _recip_o, waits=(("te", tw),), inc="ve")
                    vv = p.val("ve")
                    p.do("tensor", lambda e: e.matmul(
                        PS[6][0:64, 0:512], t_onesb[64:65, 0:64],
                        t_stageb[64:65, 0:512], start=True, stop=True),
                        waits=(("ve", vv), ("se", p.val("se"))), inc="te")
                    tw2 = p.val("te")
                    p.do("scalar", lambda e: e.copy(t_rec[0:64, 0:512],
                                                    PS[6][0:64, 0:512]),
                         waits=(("te", tw2),), inc="se")
                    sv = p.val("se")
                    p.do("vector", lambda e, hp=hp, hh=hh: e.tensor_mul(
                        t_obuf[hh * 64:(hh + 1) * 64,
                               hp * 512:(hp + 1) * 512],
                        PS[4][0:64, 0:512], t_rec[0:64, 0:512]),
                        waits=(("se", sv),), inc="ve")
                    p.bar()
            for grp in range(2):
                for mi in range(4):
                    mt = grp * 4 + mi
                    for ktp in range(8):
                        p.do("tensor", lambda e, mt=mt, ktp=ktp, mi=mi:
                             e.matmul(PS[4 + mi][:, 0:512],
                                      t_wob[:, ktp * 1024 + mt * 128:
                                            ktp * 1024 + (mt + 1) * 128],
                                      t_obuf[:, ktp * 512:(ktp + 1) * 512],
                                      start=(ktp == 0), stop=(ktp == 7)),
                             inc="te" if ktp == 7 else None)
                tw = p.val("te")
                for mi in range(4):
                    mt = grp * 4 + mi
                    p.do("vector", lambda e, mt=mt, mi=mi, qb=qb: e.tensor_add(
                        t_ln[:, mt * 512:(mt + 1) * 512],
                        xres[mt][:, 1 + qb * QB: 1 + (qb + 1) * QB],
                        PS[4 + mi][:, 0:512]),
                        waits=(("te", tw),), inc="ve")
                p.bar()
            emit_ln_stats([t_ln[:, pt * 512:(pt + 1) * 512]
                           for pt in range(8)])
            p.do("tensor", lambda e: e.matmul(
                PS[2][:, 0:512], t_onesf[0:1, 0:128], t_rstd[:],
                start=True, stop=True))
            p.do("tensor", lambda e: e.matmul(
                PS[3][:, 0:512], t_onesf[0:1, 0:128], t_mu[:],
                start=True, stop=True), inc="te")
            tw = p.val("te")
            for pt in range(8):
                p.do("vector", lambda e, pt=pt: e.tensor_mul(
                    t_ln[:, pt * 512:(pt + 1) * 512],
                    t_ln[:, pt * 512:(pt + 1) * 512], PS[2][:, 0:512]),
                    waits=(("te", tw),) if pt == 0 else (),
                    inc="ve" if pt == 7 else None)
            mv = p.val("ve")
            for pt in range(8):
                p.do("vector", lambda e, pt=pt, qb=qb: e.tensor_add(
                    xres[pt][:, 1 + qb * QB: 1 + (qb + 1) * QB],
                    t_ln[:, pt * 512:(pt + 1) * 512], PS[3][:, 0:512]),
                    waits=(("ve", mv),) if pt == 0 else (), inc="ve")
            vv = p.val("ve")
            for pt in range(8):
                p.do("sync", lambda e, pt=pt, qb=qb: e.dma_start(
                    out=x1scr[pt, :, qb * QB:(qb + 1) * QB],
                    in_=xres[pt][:, 1 + qb * QB: 1 + (qb + 1) * QB]),
                    waits=(("ve", vv),) if pt == 0 else (),
                    inc="sq", incby=16)
            p.bar()
        p.flush()

    dbg_dump("x1", [(xres[pt][:, 1:1 + NOWN],
                     dbg["x1"][pt * 128:(pt + 1) * 128, :])
                    for pt in range(8)] if "x1" in dbg else [])

    if STOP_AFTER == "p2":
        return dump_xres_and_stop()

    st2.close()
    t_qs = sb("t_qs", (128, 1), F32)
    t_ks = sb("t_ks", (128, 1), F32)

    def publish_xcol(par, ev_waits):
        for pt in range(8):
            p.do("scalar", lambda e, pt=pt: e.copy(
                t_xcol[:, pt:pt + 1], xres[pt][:, NOWN:NOWN + 1]),
                waits=ev_waits if pt == 0 else (), inc="se")
        sv = p.val("se")
        mb_dual(lambda e, par=par: e.dma_start(out=mb_x[0, par],
                                               in_=t_xcol[:]),
                lambda e, par=par: e.dma_start(out=mb_x[1, par],
                                               in_=t_xcol[:]),
                inc="sq", waits=(("se", sv),))
        return mb_flag(waits=(("sq", p.val("sq")),))

    def recv_xcol(k, par):
        mb_poll(k)
        mb_dual(lambda e, par=par: e.dma_start(out=t_xcolr[:],
                                               in_=mb_x[1, par]),
                lambda e, par=par: e.dma_start(out=t_xcolr[:],
                                               in_=mb_x[0, par]),
                inc="sq")
        sv = p.val("sq")
        p.do("vector", lambda e: e.tensor_scalar(
            out=t_xcolr[:], in0=t_xcolr[:], scalar1=t_pad[:, 0:1],
            scalar2=None, op0=ALU.mult), waits=(("sq", sv),), inc="ve")
        vv = p.val("ve")
        for pt in range(8):
            p.do("scalar", lambda e, pt=pt: e.copy(
                xres[pt][:, 0:1], t_xcolr[:, pt:pt + 1]),
                waits=(("ve", vv),) if pt == 0 else (), inc="se")
        p.bar()

    for L in range(DEPTH):
        par = L % 2
        dprobe('L%d-start' % L)
        k_ev = publish_xcol(par, ())
        recv_xcol(k_ev, par)
        p.do("sync", lambda e, L=L: e.dma_start(out=t_qs[:], in_=qs2[L]),
             inc="sq", incby=16)
        p.do("sync", lambda e, L=L: e.dma_start(out=t_ks[:], in_=ks2[L]),
             inc="sq", incby=16)
        p.bar()

        with nc.sbuf_tensor(f"wqi{L}", [128, 8 * 1024], BF16) as wqi, \
             nc.sbuf_tensor(f"xsb{L}", [128, 8 * 1025], BF16) as xsb, \
             nc.sbuf_tensor(f"tcsi{L}", [128, 1024], F32) as tcsi, \
             nc.sbuf_tensor(f"tsni{L}", [128, 1024], F32) as tsni, \
             nc.sbuf_tensor(f"qko{L}", [128, 512], BF16) as qko, \
             nc.sbuf_tensor(f"qf{L}", [128, 256], F32) as qf, \
             nc.sbuf_tensor(f"qsqb{L}", [128, 256], BF16) as qsqb, \
             nc.sbuf_tensor(f"nrm{L}", [2, 256], F32) as nrm, \
             nc.sbuf_tensor(f"t1q{L}", [128, 256], F32) as t1q, \
             nc.sbuf_tensor(f"t1qb{L}", [128, 256], BF16) as t1qb, \
             nc.sbuf_tensor(f"im1{L}", [128, 256], F32) as im1, \
             nc.sbuf_tensor(f"im2{L}", [128, 256], F32) as im2, \
             nc.sbuf_tensor(f"vblk2{L}", [128, HG], BF16) as vblk2:

            def build_xsb(h0):
                for pt in range(8):
                    p.do("scalar", lambda e, pt=pt, h0=h0: e.copy(
                        xsb[:, pt * 1025:(pt + 1) * 1025],
                        xres[pt][:, h0:h0 + 1025]),
                        waits=(("te", p.val("te")),) if pt == 0 else (),
                        inc="se")
                p.bar()

            for part in range(2):
                for kt in range(8):
                    p.do("gpsimd", lambda e, kt=kt, part=part, L=L:
                         e.dma_start(
                             out=wqi[:, kt * 1024:(kt + 1) * 1024],
                             in_=wqkv_i[L, kt * 128:(kt + 1) * 128,
                                        part * 1024:(part + 1) * 1024]),
                         waits=(("te", p.val("te")),) if kt == 0 else (),
                         inc="gq", incby=16)
                p.bar()
                for halfw in range(2):
                    build_xsb(halfw * 1024)
                    cin = (ics if part == 0 else ick)
                    sin_ = (iss if part == 0 else isk)
                    p.do("sync", lambda e, halfw=halfw, cin=cin: e.dma_start(
                        out=tcsi[:],
                        in_=cin[:, halfw * 1024:(halfw + 1) * 1024]),
                        inc="sq", incby=16)
                    p.do("sync", lambda e, halfw=halfw, sin_=sin_: e.dma_start(
                        out=tsni[:],
                        in_=sin_[:, halfw * 1024:(halfw + 1) * 1024]),
                        inc="sq", incby=16)
                    p.bar()
                    se_prev = p.val("se")
                    ve_prev = p.val("ve")
                    qslot = {0: p.val("sq"), 1: p.val("sq")}
                    for m in range(8):
                        for wr in range(4):
                            w = halfw * 4 + wr
                            psq = PS[wr % 2]
                            for kt in range(8):
                                off = 1 if kt < 4 else 0
                                p.do("tensor",
                                     lambda e, m=m, kt=kt, wr=wr, psq=psq,
                                     off=off: e.matmul(
                                         psq[:, 0:256],
                                         wqi[:, kt * 1024 + m * 128:
                                             kt * 1024 + (m + 1) * 128],
                                         xsb[:, kt * 1025 + wr * 256 + off:
                                             kt * 1025 + wr * 256 + off + 256],
                                         start=(kt == 0), stop=(kt == 7)),
                                     waits=[("se", se_prev)] if kt == 0 else (),
                                     inc="te" if kt == 7 else None)
                            tmm = p.val("te")
                            p.do("scalar", lambda e, psq=psq: e.copy(
                                qf[:], psq[:, 0:256]),
                                waits=(("te", tmm), ("ve", ve_prev)),
                                inc="se")
                            p.do("scalar", lambda e, psq=psq: e.activation(
                                out=qsqb[:], in_=psq[:, 0:256],
                                func=AF.Square), inc="se")
                            se_prev = p.val("se")
                            p.do("tensor", lambda e: e.matmul(
                                PS[4][0:2, 0:256], t_ones2[:], qsqb[:],
                                start=True, stop=True),
                                waits=(("se", se_prev),), inc="te")
                            p.do("scalar", lambda e: e.activation(
                                out=nrm[:], in_=PS[4][0:2, 0:256],
                                func=AF.Sqrt),
                                waits=(("te", p.val("te")), ("ve", ve_prev)),
                                inc="se")
                            p.do("vector", lambda e: e.tensor_scalar(
                                out=nrm[:], in0=nrm[:], scalar1=1e-12,
                                scalar2=None, op0=ALU.max),
                                waits=(("se", p.val("se")),), inc="ve")
                            p.do("vector", lambda e: e.reciprocal(
                                nrm[:], nrm[:]),
                                waits=(("ve", p.val("ve")),), inc="ve")
                            p.do("tensor", lambda e: e.matmul(
                                PS[5][:, 0:256], t_sel2[:], nrm[:],
                                start=True, stop=True),
                                waits=(("ve", p.val("ve")),), inc="te")
                            tnb = p.val("te")
                            sc = (t_qs if part == 0 else t_ks)
                            p.do("vector", lambda e, sc=sc:
                                 e.scalar_tensor_tensor(
                                     out=t1q[:], in0=qf[:],
                                     scalar=sc[:, 0:1], in1=PS[5][:, 0:256],
                                     op0=ALU.mult, op1=ALU.mult),
                                 waits=(("te", tnb),))
                            p.do("vector", lambda e, sc=sc:
                                 e.scalar_tensor_tensor(
                                     out=t1qb[:], in0=qf[:],
                                     scalar=sc[:, 0:1], in1=PS[5][:, 0:256],
                                     op0=ALU.mult, op1=ALU.mult), inc="ve")
                            p.do("tensor", lambda e: e.matmul(
                                PS[6][:, 0:256], t_perm[:], t1qb[:],
                                start=True, stop=True),
                                waits=(("ve", p.val("ve")),), inc="te")
                            tpm = p.val("te")
                            p.do("vector", lambda e, wr=wr:
                                 e.tensor_mul(
                                     im1[:], t1q[:],
                                     tcsi[:, wr * 256:(wr + 1) * 256]),
                                 waits=(("te", tpm),))
                            p.do("vector", lambda e, wr=wr:
                                 e.tensor_mul(
                                     im2[:], tsni[:, wr * 256:(wr + 1) * 256],
                                     PS[6][:, 0:256]), inc="ve")
                            sl = wr % 2
                            p.do("vector", lambda e, sl=sl: e.tensor_add(
                                qko[:, sl * 256:(sl + 1) * 256],
                                im1[:], im2[:]),
                                waits=(("sq", qslot[sl]),
                                       ("ve", p.val("ve"))), inc="ve")
                            ve_prev = p.val("ve")
                            dscr = qscr_i if part == 0 else kscr_i
                            qslot[sl] = p.do(
                                "sync", lambda e, m=m, w=w, sl=sl,
                                dscr=dscr: e.dma_start(
                                    out=dscr[m, :, w * 256:(w + 1) * 256],
                                    in_=qko[:, sl * 256:(sl + 1) * 256]),
                                waits=(("ve", ve_prev),), inc="sq", incby=16)
                            p._wait("sync", (("sq", qslot[sl]),))
            for kt in range(8):
                p.do("gpsimd", lambda e, kt=kt, L=L: e.dma_start(
                    out=wqi[:, kt * 1024:(kt + 1) * 1024],
                    in_=wqkv_i[L, kt * 128:(kt + 1) * 128, 2048:3072]),
                    waits=(("te", p.val("te")),) if kt == 0 else (),
                    inc="gq", incby=16)
            p.bar()
            for halfw in range(2):
                build_xsb(halfw * 1024)
                for tt in range(8):
                    for kt in range(8):
                        off = 1 if kt < 4 else 0
                        p.do("tensor", lambda e, tt=tt, kt=kt, off=off:
                             e.matmul(
                                 PS[4][:, 0:512],
                                 xsb[:, kt * 1025 + off + tt * 128:
                                     kt * 1025 + off + (tt + 1) * 128],
                                 wqi[:, kt * 1024: kt * 1024 + 512],
                                 start=(kt == 0), stop=(kt == 7)))
                        p.do("tensor", lambda e, tt=tt, kt=kt, off=off:
                             e.matmul(
                                 PS[5][:, 0:512],
                                 xsb[:, kt * 1025 + off + tt * 128:
                                     kt * 1025 + off + (tt + 1) * 128],
                                 wqi[:, kt * 1024 + 512: kt * 1024 + 1024],
                                 start=(kt == 0), stop=(kt == 7)),
                             inc="te" if kt == 7 else None)
                    tv = p.val("te")
                    p.do("vector", lambda e: e.memset(
                        vblk2.rearrange("p (h c) -> p h c", h=H)[:, :, 64:65],
                        1.0), waits=(("te", tv),), inc="ve")
                    vv = p.val("ve")
                    for half, pv in ((0, PS[4]), (1, PS[5])):
                        p.do("scalar", lambda e, half=half, pv=pv: e.copy(
                            vblk2.rearrange("p (h c) -> p h c", h=H)[
                                :, half * 8:(half + 1) * 8, 0:64],
                            pv[:, 0:512].rearrange("p (h c) -> p h c", h=8)),
                            waits=(("ve", vv),), inc="se")
                    sv = p.val("se")
                    p.do("sync", lambda e, halfw=halfw, tt=tt: e.dma_start(
                        out=vscr_i[2 + halfw * 8 + tt], in_=vblk2[:]),
                        waits=(("se", sv),), inc="sq", incby=16)
                    p.bar()
            p.flush()

        dprobe('L%d-postqkv' % L)
        for m in range(8):
            mb_dual(lambda e, m=m, par=par: e.dma_start(
                out=mb_k[0, par, m], in_=kscr_i[m, :, NOWN - WIN: NOWN]),
                lambda e, m=m, par=par: e.dma_start(
                out=mb_k[1, par, m], in_=kscr_i[m, :, NOWN - WIN: NOWN]),
                inc="sq")
        for c in range(2):
            mb_dual(lambda e, par=par, c=c: e.dma_start(
                out=mb_v[0, par, c], in_=vscr_i[16 + c]),
                lambda e, par=par, c=c: e.dma_start(
                out=mb_v[1, par, c], in_=vscr_i[16 + c]),
                inc="sq")
        k_kv = mb_flag(waits=(("sq", p.val("sq")),))
        k_ffn = [None]

        with nc.sbuf_tensor(f"woib{L}", [128, 8 * 1024], BF16) as woib, \
             nc.sbuf_tensor(f"vwin{L}", [128, 4 * HG], BF16) as vwin, \
             nc.sbuf_tensor(f"kwin{L}", [128, 8 * 512], BF16) as kwin, \
             nc.sbuf_tensor(f"qwin{L}", [128, 8 * 256], BF16) as qwin, \
             nc.sbuf_tensor(f"nmx{L}", [128, 2], F32) as nmx, \
             nc.sbuf_tensor(f"nmxb{L}", [128, 2], BF16) as nmxb, \
             nc.sbuf_tensor(f"ptile{L}", [128, 4 * 256], BF16) as ptile, \
             nc.sbuf_tensor(f"orec{L}", [128, 256], F32) as orec, \
             nc.sbuf_tensor(f"orecb{L}", [128, 256], BF16) as orecb, \
             nc.sbuf_tensor(f"obuf2{L}", [128, 8 * 256], BF16) as obuf2:

            for kt in range(8):
                p.do("gpsimd", lambda e, kt=kt, L=L: e.dma_start(
                    out=woib[:, kt * 1024:(kt + 1) * 1024],
                    in_=wo_i[L, kt * 128:(kt + 1) * 128, :]),
                    inc="gq", incby=16)
            p.bar()
            for w in list(range(1, NWIN)) + [0]:
                dprobe('L%d-w%d' % (L, w))
                tprev = p.val("te")
                p.do("sync", lambda e, w=w: e.dma_start(
                    out=qwin.rearrange("p (a c) -> p a c", a=8),
                    in_=qscr_i[:, :, w * 256:(w + 1) * 256].rearrange(
                        "a p c -> p a c")),
                    waits=(("te", tprev),), inc="sq", incby=16)
                if w == 0:
                    mb_poll(k_kv)
                    for m in range(8):
                        mb_dual(lambda e, par=par, m=m: e.dma_start(
                            out=kwin[:, m * 512: m * 512 + WIN],
                            in_=mb_k[1, par, m]),
                            lambda e, par=par, m=m: e.dma_start(
                            out=kwin[:, m * 512: m * 512 + WIN],
                            in_=mb_k[0, par, m]),
                            inc="sq")
                    p.do("sync", lambda e: e.dma_start(
                        out=kwin.rearrange("p (a c) -> p a c", a=8)[
                            :, :, WIN:512],
                        in_=kscr_i[:, :, 0:WIN].rearrange("a p c -> p a c")),
                        inc="sq", incby=16)
                    for c in range(2):
                        mb_dual(lambda e, par=par, c=c: e.dma_start(
                            out=vwin[:, c * HG:(c + 1) * HG],
                            in_=mb_v[1, par, c]),
                            lambda e, par=par, c=c: e.dma_start(
                            out=vwin[:, c * HG:(c + 1) * HG],
                            in_=mb_v[0, par, c]),
                            inc="sq")
                    sv = p.val("sq")
                    for m in range(8):
                        p.do("vector", lambda e, m=m: e.tensor_scalar(
                            out=kwin[:, m * 512: m * 512 + WIN],
                            in0=kwin[:, m * 512: m * 512 + WIN],
                            scalar1=t_pad[:, 0:1], scalar2=None,
                            op0=ALU.mult),
                            waits=(("sq", sv),) if m == 0 else ())
                    for c in range(2):
                        p.do("vector", lambda e, c=c: e.tensor_scalar(
                            out=vwin[:, c * HG:(c + 1) * HG],
                            in0=vwin[:, c * HG:(c + 1) * HG],
                            scalar1=t_pad[:, 0:1], scalar2=None,
                            op0=ALU.mult))
                    p.do("vector", lambda e: e.nop(), inc="ve")
                    p.do("sync", lambda e, w=w: e.dma_start(
                        out=vwin.rearrange("p (a c) -> p a c", a=4)[:, 2:4],
                        in_=vscr_i[2: 4].rearrange("a p c -> p a c")),
                        inc="sq", incby=16)
                else:
                    p.do("sync", lambda e, w=w: e.dma_start(
                        out=kwin.rearrange("p (a c) -> p a c", a=8),
                        in_=kscr_i[:, :, (w - 1) * 256:(w + 1) * 256]
                        .rearrange("a p c -> p a c")), inc="sq", incby=16)
                    p.do("sync", lambda e, w=w: e.dma_start(
                        out=vwin.rearrange("p (a c) -> p a c", a=4),
                        in_=vscr_i[2 * w: 4 + 2 * w].rearrange(
                            "a p c -> p a c")), inc="sq", incby=16)
                p.bar()
                mvar = 0 if w == 0 else 1
                se_prev = p.val("se")
                ve_prev = p.val("ve")
                te_prev = p.val("te")
                for hp in range(8):
                    for hh in range(2):
                        h = 2 * hp + hh
                        r0, r1 = hh * 64, (hh + 1) * 64
                        for qt in range(2):
                            nk = 384 if qt == 0 else 512
                            psq = PS[qt]
                            p.do("tensor", lambda e, hp=hp, r0=r0, r1=r1,
                                 qt=qt, nk=nk, psq=psq: e.matmul(
                                     psq[:, 0:nk],
                                     qwin[r0:r1, hp * 256 + qt * 128:
                                          hp * 256 + (qt + 1) * 128],
                                     kwin[r0:r1, hp * 512: hp * 512 + nk],
                                     start=True, stop=True),
                                 waits=[("se", se_prev), ("ve", ve_prev)]
                                 if qt == 0 else (), inc="te")
                            tmm = p.val("te")
                            p.do("vector", lambda e, nk=nk, psq=psq:
                                 e.tensor_add(
                                     psq[:, nk - 128:nk],
                                     psq[:, nk - 128:nk], t_m1[:]),
                                 waits=(("te", tmm), ("te", te_prev)),
                                 inc="ve")
                            if w == 0:
                                p.do("vector", lambda e, nk=nk, psq=psq:
                                     e.tensor_add(
                                         psq[:, 0:256], psq[:, 0:256],
                                         t_w0pm[:, 0:256]), inc="ve")
                            p.do("vector", lambda e, qt=qt, nk=nk, psq=psq:
                                 e.reduce_max(
                                     nmx[:, qt:qt + 1], psq[:, 0:nk],
                                     axis=AX.X, negate=True),
                                 waits=(("ve", p.val("ve")),), inc="ve")
                        p.do("vector", lambda e: e.tensor_copy(
                            nmxb[:], nmx[:]),
                            waits=(("ve", p.val("ve")),), inc="ve")
                        vv = p.val("ve")
                        for qt in range(2):
                            p.do("tensor", lambda e, qt=qt: e.matmul(
                                PS[2][64:65, qt * 128:(qt + 1) * 128],
                                nmxb[:, qt:qt + 1], t_identb[:],
                                start=True, stop=True,
                                tile_position=(0, 64)),
                                waits=(("ve", vv),) if qt == 0
                                else (("se", p.val("se")),),
                                inc="te")
                            tw = p.val("te")
                            p.do("scalar", lambda e, qt=qt: e.mul(
                                t_stageb[64:65, qt * 128:(qt + 1) * 128],
                                PS[2][64:65, qt * 128:(qt + 1) * 128], 1.0),
                                waits=(("te", tw),), inc="se")
                        sv = p.val("se")
                        expd, avd = {}, {}
                        for kt in range(4):
                            psS = PS[kt % 2]
                            wts = [("se", sv)]
                            if kt >= 2:
                                wts.append(("se", expd[kt - 2]))
                            p.do("tensor", lambda e, hp=hp, r0=r0, r1=r1,
                                 kt=kt, psS=psS: e.matmul(
                                     psS[:, 0:256],
                                     kwin[r0:r1, hp * 512 + kt * 128:
                                          hp * 512 + (kt + 1) * 128],
                                     qwin[r0:r1, hp * 256:
                                          hp * 256 + 256],
                                     start=True, stop=False), waits=wts)
                            p.do("tensor", lambda e, psS=psS: e.matmul(
                                psS[:, 0:256], t_onesb[64:65, 0:128],
                                t_stageb[64:65, 0:256],
                                start=False, stop=True), inc="te")
                            tmm = p.val("te")
                            ew = [("te", tmm)]
                            if kt < 2:
                                p.do("vector", lambda e, kt=kt, mvar=mvar,
                                     psS=psS: e.tensor_add(
                                         psS[:, 0:256], psS[:, 0:256],
                                         t_m01n[:, (mvar * 2 + kt) * 256:
                                                (mvar * 2 + kt + 1) * 256]),
                                     waits=(("te", tmm),), inc="ve")
                                ew = [("ve", p.val("ve"))]
                            if kt == 2:
                                p.do("vector", lambda e, psS=psS:
                                     e.tensor_add(psS[:, 0:128],
                                                  psS[:, 0:128], t_m2T[:]),
                                     waits=(("te", tmm),), inc="ve")
                                ew = [("ve", p.val("ve"))]
                            if kt == 3:
                                p.do("vector", lambda e, psS=psS:
                                     e.tensor_add(psS[:, 128:256],
                                                  psS[:, 128:256], t_m2T[:]),
                                     waits=(("te", tmm),), inc="ve")
                                ew = [("ve", p.val("ve"))]
                            if kt >= 2:
                                ew.append(("te", avd[kt - 2]))
                            psl = ptile[:, kt * 256:(kt + 1) * 256]
                            if kt < 3:
                                p.do("scalar", lambda e, psl=psl, psS=psS:
                                     e.activation(out=psl, in_=psS[:, 0:256],
                                                  func=AF.Exp, scale=8.0),
                                     waits=ew, inc="se")
                            else:
                                p.do("scalar", lambda e, psl=psl, psS=psS:
                                     e.activation(out=psl[:, 128:256],
                                                  in_=psS[:, 128:256],
                                                  func=AF.Exp, scale=8.0),
                                     waits=ew, inc="se")
                                p.do("vector", lambda e, psl=psl: e.memset(
                                    psl[:, 0:128], 0.0), inc="ve")
                            expd[kt] = p.val("se")
                            mwts = [("se", expd[kt])]
                            if kt == 3:
                                mwts.append(("ve", p.val("ve")))
                            p.do("tensor", lambda e, kt=kt, hp=hp, hh=hh,
                                 psl=psl: e.matmul(
                                     PS[4][0:65, 0:256],
                                     vwin[:, kt * HG + (2 * hp + hh) * 65:
                                          kt * HG + (2 * hp + hh + 1) * 65],
                                     psl, start=(kt == 0), stop=(kt == 3)),
                                 waits=mwts, inc="te")
                            avd[kt] = p.val("te")
                        tw = p.val("te")

                        def _recip_i(e):
                            with nc.allow_low_precision(
                                    reason="recip feeds bf16 matmul; same "
                                           "precision as prior copy-cast"):
                                e.reciprocal(orecb[64:65, 0:256],
                                             PS[4][64:65, 0:256])
                        p.do("vector", _recip_i,
                             waits=(("te", tw),), inc="ve")
                        vv = p.val("ve")
                        p.do("tensor", lambda e: e.matmul(
                            PS[6][0:64, 0:256], t_onesb[64:65, 0:64],
                            orecb[64:65, 0:256], start=True, stop=True),
                            waits=(("ve", vv), ("se", p.val("se"))),
                            inc="te")
                        tw2 = p.val("te")
                        p.do("scalar", lambda e: e.copy(
                            orec[0:64, 0:256], PS[6][0:64, 0:256]),
                            waits=(("te", tw2),), inc="se")
                        sv2 = p.val("se")
                        p.do("vector", lambda e, hp=hp, hh=hh: e.tensor_mul(
                            obuf2[hh * 64:(hh + 1) * 64,
                                  hp * 256:(hp + 1) * 256],
                            PS[4][0:64, 0:256], orec[0:64, 0:256]),
                            waits=(("se", sv2),), inc="ve")
                        se_prev = p.val("se")
                        ve_prev = p.val("ve")
                        te_prev = p.val("te")
                for grp in range(2):
                    for mi in range(4):
                        mt = grp * 4 + mi
                        for ktp in range(8):
                            p.do("tensor", lambda e, mt=mt, ktp=ktp, mi=mi:
                                 e.matmul(
                                     PS[4 + mi][:, 0:256],
                                     woib[:, ktp * 1024 + mt * 128:
                                          ktp * 1024 + (mt + 1) * 128],
                                     obuf2[:, ktp * 256:(ktp + 1) * 256],
                                     start=(ktp == 0), stop=(ktp == 7)),
                                 waits=[("ve", ve_prev)]
                                 if (mi == 0 and ktp == 0 and grp == 0)
                                 else (), inc="te" if ktp == 7 else None)
                    tw = p.val("te")
                    for mi in range(4):
                        mt = grp * 4 + mi
                        p.do("vector", lambda e, mt=mt, mi=mi, w=w:
                             e.tensor_add(
                                 xres[mt][:, 1 + w * 256: 1 + (w + 1) * 256],
                                 xres[mt][:, 1 + w * 256: 1 + (w + 1) * 256],
                                 PS[4 + mi][:, 0:256]),
                             waits=(("te", tw),), inc="ve")
                    p.bar()
                if w == NWIN - 1:
                    k_ffn[0] = publish_xcol(par, (("ve", p.val("ve")),))
            p.flush()

        if STOP_AFTER == ("l%da" % L):
            return dump_xres_and_stop()

        recv_xcol(k_ffn[0], par)
        with nc.sbuf_tensor(f"xnb{L}", [128, 8 * 2048], BF16) as xnb, \
             nc.sbuf_tensor(f"grow{L}", [1, D], F32) as grow, \
             nc.sbuf_tensor(f"brow{L}", [1, D], F32) as brow, \
             nc.sbuf_tensor(f"b2row{L}", [1, D], BF16) as b2row, \
             nc.sbuf_tensor(f"b1ac{L}", [128, NHT], F32) as b1ac, \
             nc.sbuf_tensor(f"b1gc{L}", [128, NHT], F32) as b1gc, \
             nc.sbuf_tensor(f"w1at{L}", [128, 8 * 128], BF16) as w1at, \
             nc.sbuf_tensor(f"w1gt{L}", [128, 8 * 128], BF16) as w1gt, \
             nc.sbuf_tensor(f"gel{L}", [128, 512], BF16) as gel, \
             nc.sbuf_tensor(f"lnX{L}", [128, 512], F32) as lnX, \
             nc.sbuf_tensor(f"th{L}", [128, HHALF * 2048], BF16) as th, \
             nc.sbuf_tensor(f"w2t{L}", [128, HHALF * 1024], BF16) as w2t:

            p.do("sync", lambda e, L=L: e.dma_start(
                out=grow[:], in_=lng[L:L + 1, :]), inc="sq", incby=16)
            p.do("sync", lambda e, L=L: e.dma_start(
                out=brow[:], in_=lnb[L:L + 1, :]), inc="sq", incby=16)
            p.do("gpsimd", lambda e, L=L: e.dma_start(
                out=b2row[:], in_=b2_i[L:L + 1, :]), inc="gq", incby=16)
            p.do("sync", lambda e, L=L: e.dma_start(
                out=b1ac[:], in_=b1a_i[L]), inc="sq", incby=16)
            p.do("sync", lambda e, L=L: e.dma_start(
                out=b1gc[:], in_=b1g_i[L]), inc="sq", incby=16)
            p.bar()

            for qb in range(NQB):
                shsrc = []
                for pt in range(8):
                    off = 1 if pt < 4 else 0
                    shsrc.append(xres[pt][:, off + qb * QB:
                                          off + (qb + 1) * QB])
                emit_ln_stats(shsrc)
                for pt in range(8):
                    p.do("tensor", lambda e, pt=pt: e.matmul(
                        PS[2][:, 0:512], grow[0:1, pt * 128:(pt + 1) * 128],
                        t_rstd[:], start=True, stop=True),
                        waits=(("ve", p.val("ve")),) if pt > 0 else ())
                    p.do("tensor", lambda e, pt=pt: e.matmul(
                        PS[3][:, 0:512], grow[0:1, pt * 128:(pt + 1) * 128],
                        t_mu[:], start=True, stop=False))
                    p.do("tensor", lambda e, pt=pt: e.matmul(
                        PS[3][:, 0:512], brow[0:1, pt * 128:(pt + 1) * 128],
                        t_onesf[0:1, 0:512], start=False, stop=True),
                        inc="te")
                    tw = p.val("te")
                    p.do("vector", lambda e, pt=pt, sl=shsrc[pt]:
                         e.tensor_mul(lnX[:], sl, PS[2][:, 0:512]),
                         waits=(("te", tw),), inc="ve")
                    p.do("vector", lambda e, pt=pt, qb=qb: e.tensor_add(
                        xnb[:, pt * 2048 + qb * 512:
                            pt * 2048 + (qb + 1) * 512],
                        lnX[:], PS[3][:, 0:512]),
                        waits=(("ve", p.val("ve")),), inc="ve")
                p._wait("tensor", (("ve", p.val("ve")),))

            for hf in range(2):
                for htl in range(HHALF):
                    ht = hf * HHALF + htl
                    p.do("gpsimd", lambda e, ht=ht, L=L: e.dma_start(
                        out=w1at.rearrange("p (k c) -> p k c", k=8),
                        in_=W1a[L, :, ht * 128:(ht + 1) * 128].rearrange(
                            "(k p) c -> p k c", p=128)), inc="gq", incby=16)
                    p.do("gpsimd", lambda e, ht=ht, L=L: e.dma_start(
                        out=w1gt.rearrange("p (k c) -> p k c", k=8),
                        in_=W1g[L, :, ht * 128:(ht + 1) * 128].rearrange(
                            "(k p) c -> p k c", p=128)), inc="gq", incby=16)
                    p.bar()
                    for ch in range(4):
                        chw = [("se", p.val("se")), ("ve", p.val("ve"))] \
                            if ch > 0 else ()
                        for kt in range(8):
                            p.do("tensor", lambda e, kt=kt, ch=ch: e.matmul(
                                PS[0][:, 0:512],
                                w1at[:, kt * 128:(kt + 1) * 128],
                                xnb[:, kt * 2048 + ch * 512:
                                    kt * 2048 + (ch + 1) * 512],
                                start=(kt == 0), stop=(kt == 7)),
                                waits=chw if kt == 0 else ())
                            p.do("tensor", lambda e, kt=kt, ch=ch: e.matmul(
                                PS[1][:, 0:512],
                                w1gt[:, kt * 128:(kt + 1) * 128],
                                xnb[:, kt * 2048 + ch * 512:
                                    kt * 2048 + (ch + 1) * 512],
                                start=(kt == 0), stop=(kt == 7)),
                                inc="te" if kt == 7 else None)
                        tw = p.val("te")
                        p.do("scalar", lambda e, ht=ht: e.activation(
                            out=gel[:], in_=PS[1][:, 0:512], func=AF.Gelu,
                            bias=b1gc[:, ht:ht + 1]),
                            waits=(("te", tw),), inc="se")
                        sv = p.val("se")
                        p.do("vector", lambda e, htl=htl, ch=ch, ht=ht:
                             e.scalar_tensor_tensor(
                                 out=th[:, htl * 2048 + ch * 512:
                                        htl * 2048 + (ch + 1) * 512],
                                 in0=PS[0][:, 0:512],
                                 scalar=b1ac[:, ht:ht + 1], in1=gel[:],
                                 op0=ALU.add, op1=ALU.mult),
                             waits=(("se", sv),), inc="ve")
                    p.bar()
                for ktl in range(HHALF):
                    kt = hf * HHALF + ktl
                    p.do("gpsimd", lambda e, kt=kt, ktl=ktl, L=L: e.dma_start(
                        out=w2t[:, ktl * 1024:(ktl + 1) * 1024],
                        in_=W2p[L, kt * 128:(kt + 1) * 128, :]),
                        inc="gq", incby=16)
                p.bar()
                for ch in range(4):
                    for grp in range(2):
                        gw = [("ve", p.val("ve"))] if (ch or grp) else ()
                        for mi in range(4):
                            mt = grp * 4 + mi
                            for ktl in range(HHALF):
                                last = (ktl == HHALF - 1) and (hf == 1)
                                p.do("tensor", lambda e, mt=mt, ktl=ktl,
                                     ch=ch, mi=mi, last=last: e.matmul(
                                         PS[4 + mi][:, 0:512],
                                         w2t[:, ktl * 1024 + mt * 128:
                                             ktl * 1024 + (mt + 1) * 128],
                                         th[:, ktl * 2048 + ch * 512:
                                            ktl * 2048 + (ch + 1) * 512],
                                         start=(ktl == 0), stop=last),
                                     waits=gw if (mi == 0 and ktl == 0)
                                     else (), inc="te" if last else None)
                            if hf == 0:
                                p.do("tensor", lambda e, mt=mt, mi=mi:
                                     e.matmul(
                                         PS[4 + mi][:, 0:512],
                                         b2row[0:1, mt * 128:(mt + 1) * 128],
                                         t_onesb[0:1, 0:512],
                                         start=False, stop=True), inc="te")
                        tw = p.val("te")
                        for mi in range(4):
                            mt = grp * 4 + mi
                            p.do("vector", lambda e, mt=mt, mi=mi, ch=ch:
                                 e.tensor_add(
                                     xres[mt][:, 1 + ch * 512:
                                              1 + (ch + 1) * 512],
                                     xres[mt][:, 1 + ch * 512:
                                              1 + (ch + 1) * 512],
                                     PS[4 + mi][:, 0:512]),
                                 waits=(("te", tw),), inc="ve")
                p.bar()
            p.flush()

        if STOP_AFTER == ("l%d" % L):
            return dump_xres_and_stop()
        if ("xl%d" % L) in dbg:
            dbg_dump("xl%d" % L,
                     [(xres[pt][:, 1:1 + NOWN],
                       dbg["xl%d" % L][pt * 128:(pt + 1) * 128, :])
                      for pt in range(8)])

    with nc.sbuf_tensor("x1b", [128, 8 * 512], F32) as x1b, \
         nc.sbuf_tensor("t_ln4", [128, 8 * 512], F32) as t_ln, \
         nc.sbuf_tensor("ytm", [128, 1024], F32) as ytm:
        for qb in range(NQB):
            p.do("sync", lambda e, qb=qb: e.dma_start(
                out=x1b.rearrange("p (a c) -> p a c", a=8),
                in_=x1scr[:, :, qb * QB:(qb + 1) * QB].rearrange(
                    "a p c -> p a c")), inc="sq", incby=16)
            p.bar()
            for pt in range(8):
                p.do("vector", lambda e, pt=pt, qb=qb: e.tensor_add(
                    t_ln[:, pt * 512:(pt + 1) * 512],
                    xres[pt][:, 1 + qb * QB: 1 + (qb + 1) * QB],
                    x1b[:, pt * 512:(pt + 1) * 512]), inc="ve")
            p.bar()
            emit_ln_stats([t_ln[:, pt * 512:(pt + 1) * 512]
                           for pt in range(8)])
            p.do("tensor", lambda e: e.matmul(
                PS[2][:, 0:512], t_onesf[0:1, 0:128], t_rstd[:],
                start=True, stop=True))
            p.do("tensor", lambda e: e.matmul(
                PS[3][:, 0:512], t_onesf[0:1, 0:128], t_mu[:],
                start=True, stop=True), inc="te")
            tw = p.val("te")
            for pt in range(8):
                p.do("vector", lambda e, pt=pt: e.tensor_mul(
                    t_ln[:, pt * 512:(pt + 1) * 512],
                    t_ln[:, pt * 512:(pt + 1) * 512], PS[2][:, 0:512]),
                    waits=(("te", tw),) if pt == 0 else (),
                    inc="ve" if pt == 7 else None)
            mv = p.val("ve")
            for pt in range(8):
                p.do("vector", lambda e, pt=pt: e.tensor_add(
                    t_ln[:, pt * 512:(pt + 1) * 512],
                    t_ln[:, pt * 512:(pt + 1) * 512], PS[3][:, 0:512]),
                    waits=(("ve", mv),) if pt == 0 else (), inc="ve")
            p.bar()
            for tt in range(4):
                for pt in range(8):
                    pb = PS[pt % 2]
                    wl = [("se", p.val("se") - 1)] if (tt or pt) else ()
                    p.do("tensor", lambda e, tt=tt, pt=pt, pb=pb: e.transpose(
                        pb[:, 0:128],
                        t_ln[:, pt * 512 + tt * 128:
                             pt * 512 + (tt + 1) * 128],
                        t_ident[:]), waits=wl, inc="te")
                    tw = p.val("te")
                    p.do("scalar", lambda e, pt=pt, pb=pb: e.copy(
                        ytm[:, pt * 128:(pt + 1) * 128], pb[:, 0:128]),
                        waits=(("te", tw),), inc="se")
                sv = p.val("se")
                p.do("sync", lambda e, qb=qb, tt=tt: e.dma_start(
                    out=y[qb * QB + tt * 128: qb * QB + (tt + 1) * 128, :],
                    in_=ytm[:]), waits=(("se", sv),), inc="sq", incby=16)
                p.bar()
        p.bar()
        p.flush()

    st.close()
    return nc



_CACHE = {}


def kernel(**inputs):
    if "nc" not in _CACHE:
        _CACHE["nc"] = build_program(debug=())
    nc = _CACHE["nc"]

    const = host_const_tables()
    wts = host_weight_layouts(inputs)
    xall = np.asarray(inputs["L0_attn_x"], dtype=np.float32)
    nonce = np.random.randint(1, 2 ** 31 - 100, dtype=np.uint32)
    fv = (nonce + np.arange(1, 65, dtype=np.uint32)).reshape(1, 64)

    in_maps = []
    for core in range(NCORE):
        be, hf = core // 2, core % 2
        pc = host_percore_tables(core, const)
        m = {
            "xfull": np.ascontiguousarray(xall[be]),
            "xown": np.ascontiguousarray(xall[be, hf * NOWN:(hf + 1) * NOWN]),
            "fvals": fv,
        }
        for k in ("ocos", "osin", "permB", "ident", "identb", "onesf",
                  "onesb", "ones2", "sel2", "posrel64", "m1stair",
                  "m2stairT"):
            m[k] = const[k]
        for k in ("ocos_q", "osin_q", "ics", "iss", "ick", "isk", "poskrel",
                  "omaskflag", "ocausal", "w0prevmask", "mask01prev",
                  "mask01neg",
                  "padmask", "slot"):
            m[k] = pc[k]
        for k in ("wqkv_o", "wo_o", "logxi_r", "ebits_r", "wqkv_i", "wo_i",
                  "qs2", "ks2", "lng", "lnb", "W1a", "W1g", "b1a", "b1g",
                  "W2p", "b2"):
            m[k] = wts[k]
        in_maps.append(m)

    res = run_bass_kernel_spmd(nc, in_maps, core_ids=list(range(NCORE)))
    out = np.empty((B, N, D), dtype=np.float32)
    for core in range(NCORE):
        be, hf = core // 2, core % 2
        out[be, hf * NOWN:(hf + 1) * NOWN] = res.results[core]["y"]
    return out


# version 06
# speedup vs baseline: 1.2444x; 1.2444x over previous
import contextlib
import numpy as np
import ml_dtypes
import concourse.bass as bass
import concourse.mybir as mybir
from concourse.bass_utils import run_bass_kernel_spmd

F32 = mybir.dt.float32
BF16 = mybir.dt.bfloat16
U32 = mybir.dt.uint32
AF = mybir.ActivationFunctionType
ALU = mybir.AluOpType
AX = mybir.AxisListType

NCORE = 8
B, N, D = 4, 4096, 1024
H, DH = 16, 64
NOWN = 2048
QB = 512
NQB = NOWN // QB
KT = N // 128
WIN = 256
NWIN = NOWN // WIN
DEPTH = 6
DI = 2730
DIP = 2816
NHT = DIP // 128
HHALF = NHT // 2
NEG = -1.0e30
HG = H * 65
ENGS = ("tensor", "vector", "scalar", "sync", "gpsimd")
BF = ml_dtypes.bfloat16

STOP_AFTER = None
SIM_SKIP_UNINIT_DUMP = False
MB_SELF = False



def host_const_tables():
    t = {}
    pos = np.arange(N, dtype=np.float64)
    inv_freq = 10000.0 ** (-(np.arange(0, DH, 2, dtype=np.float64) / DH))
    theta = pos[None, :] * inv_freq[:, None]
    cosk, sink = np.cos(theta), np.sin(theta)
    t["ocos"] = np.tile(cosk, (4, 1)).astype(np.float32)
    t["osin"] = np.concatenate([-sink, sink, -sink, sink], 0).astype(np.float32)

    half = np.arange(0, DH, 2, dtype=np.float64)
    inv2 = 1.0 / 10000.0 ** (half / DH)
    tt = np.arange(N, dtype=np.float64)
    fr = tt[None, :] * inv2[:, None]
    base_scale = (half + 0.4 * DH) / (1.4 * DH)
    power = (tt - N // 2) / (WIN // 2)
    scale = base_scale[:, None] ** power[None, :]
    cos64 = np.concatenate([np.cos(fr)] * 2, 0)
    ssin64 = np.concatenate([-np.sin(fr), np.sin(fr)], 0)
    sc64 = np.concatenate([scale] * 2, 0)
    t["ics_full"] = np.tile(cos64 * sc64, (2, 1)).astype(np.float32)
    t["iss_full"] = np.tile(ssin64 * sc64, (2, 1)).astype(np.float32)
    t["ick_full"] = np.tile(cos64 / sc64, (2, 1)).astype(np.float32)
    t["isk_full"] = np.tile(ssin64 / sc64, (2, 1)).astype(np.float32)

    P = np.zeros((128, 128), dtype=np.float32)
    for blk in range(2):
        b0 = blk * 64
        for i in range(32):
            P[b0 + 32 + i, b0 + i] = 1.0
            P[b0 + i, b0 + 32 + i] = 1.0
    t["permB"] = P
    t["ident"] = np.eye(128, dtype=np.float32)
    t["identb"] = np.eye(128, dtype=np.float32)
    t["onesf"] = np.ones((128, 512), dtype=np.float32)
    t["onesb"] = np.ones((128, 512), dtype=np.float32)
    o2 = np.zeros((128, 2), dtype=np.float32)
    o2[0:64, 0] = 1.0
    o2[64:128, 1] = 1.0
    t["ones2"] = o2
    s2 = np.zeros((2, 128), dtype=np.float32)
    s2[0, 0:64] = 1.0
    s2[1, 64:128] = 1.0
    t["sel2"] = s2
    pr = np.zeros((128, QB), dtype=np.float32)
    pr[64, :] = np.arange(QB, dtype=np.float32)
    t["posrel64"] = pr
    m1 = np.where(np.arange(128)[None, :] <= np.arange(128)[:, None], 0.0, NEG)
    t["m1stair"] = m1.astype(np.float32)
    m2 = np.where(np.arange(128)[:, None] <= np.arange(128)[None, :], 0.0, NEG)
    t["m2stairT"] = m2.astype(np.float32)
    return t


def host_percore_tables(core, const):
    be, hf = core // 2, core % 2
    own0 = hf * NOWN
    t = {}
    t["ocos_q"] = np.ascontiguousarray(const["ocos"][:, own0:own0 + NOWN])
    t["osin_q"] = np.ascontiguousarray(const["osin"][:, own0:own0 + NOWN])
    for nm in ("ics", "iss", "ick", "isk"):
        t[nm] = np.ascontiguousarray(const[nm + "_full"][:, own0:own0 + NOWN])

    pk = np.zeros((NQB, 128, KT), dtype=np.float32)
    j = (np.arange(128)[:, None] + 128 * np.arange(KT)[None, :]).astype(np.float64)
    for qb in range(NQB):
        pk[qb] = (j - (own0 + qb * QB)).astype(np.float32)
    t["poskrel"] = pk

    mf = np.zeros((NQB, KT), dtype=np.float32)
    for qb in range(NQB):
        qend = own0 + qb * QB + QB - 1
        for kt in range(KT):
            if 128 * kt > qend:
                mf[qb, kt] = NEG
    t["omaskflag"] = mf

    oc = np.zeros((NQB, 8, 128, QB), dtype=np.float32)
    for qb in range(NQB):
        iabs = own0 + qb * QB + np.arange(QB)
        for s in range(8):
            kt = (4 * qb + s) if s < 4 else (16 + 4 * qb + (s - 4))
            if mf[qb, kt] == NEG:
                continue
            jabs = 128 * kt + np.arange(128)
            stair = np.where(jabs[:, None] <= iabs[None, :], 0.0, NEG)
            if np.any(stair == NEG):
                oc[qb, s] = stair
    t["ocausal"] = oc.astype(BF)

    w0m = np.zeros((128, 512), dtype=np.float32)
    if hf == 0:
        w0m[:, 0:256] = NEG
    t["w0prevmask"] = w0m

    q = np.arange(256)
    mm = np.zeros((2, 2, 128, 256), dtype=np.float32)
    for kt in range(2):
        jr = 128 * kt + np.arange(128)
        tooold = (jr[:, None] >= q[None, :]).astype(np.float32)
        mm[1, kt] = tooold
        mm[0, kt] = tooold if hf == 1 else 0.0
    t["mask01prev"] = mm.astype(BF)
    t["mask01neg"] = np.where(mm == 1.0, 0.0, NEG).astype(np.float32)

    t["padmask"] = np.full((128, 1), 0.0 if MB_SELF else float(hf),
                           dtype=np.float32)
    t["slot"] = np.array([[core & 1]], dtype=np.uint32)
    return t


def host_weight_layouts(inputs):
    w = {}
    wqkv = np.asarray(inputs["L0_attn_w_qkv"], dtype=np.float32)
    perm = np.concatenate([np.arange(0, DH, 2), np.arange(1, DH, 2)])
    wp = wqkv.copy()
    for part in range(2):
        for h in range(H):
            c0 = part * D + h * DH
            wp[:, c0:c0 + DH] = wqkv[:, c0 + perm]
    w["wqkv_o"] = np.ascontiguousarray(wp)
    w["wo_o"] = np.ascontiguousarray(np.asarray(inputs["L0_attn_w_o"], np.float32))
    lx = np.asarray(inputs["L0_attn_log_xi"], np.float32)
    w["logxi_r"] = np.ascontiguousarray(lx.reshape(1, H))
    et = -(np.arange(H) % 4 % 2).astype(np.float32)
    w["ebits_r"] = np.ascontiguousarray(et.reshape(1, H))

    w["wqkv_i"] = np.ascontiguousarray(np.asarray(inputs["L0_ffn_Wqkv"], np.float32))
    w["wo_i"] = np.ascontiguousarray(np.asarray(inputs["L0_ffn_Wo"], np.float32))
    qs = np.asarray(inputs["L0_ffn_q_scale"], np.float32)
    ks = np.asarray(inputs["L0_ffn_k_scale"], np.float32)
    w["qs2"] = np.ascontiguousarray(np.tile(qs, (1, 2)).reshape(DEPTH, 128, 1))
    w["ks2"] = np.ascontiguousarray(np.tile(ks, (1, 2)).reshape(DEPTH, 128, 1))
    w["lng"] = np.ascontiguousarray(np.asarray(inputs["L0_ffn_ln_g"], np.float32))
    w["lnb"] = np.ascontiguousarray(np.asarray(inputs["L0_ffn_ln_b"], np.float32))

    W1 = np.asarray(inputs["L0_ffn_W1"], np.float32)
    b1 = np.asarray(inputs["L0_ffn_b1"], np.float32)
    W2 = np.asarray(inputs["L0_ffn_W2"], np.float32)
    W1a = np.zeros((DEPTH, D, DIP), dtype=np.float32)
    W1g = np.zeros((DEPTH, D, DIP), dtype=np.float32)
    W1a[:, :, :DI] = W1[:, :, :DI]
    W1g[:, :, :DI] = W1[:, :, DI:]
    b1a = np.zeros((DEPTH, DIP), dtype=np.float32)
    b1g = np.zeros((DEPTH, DIP), dtype=np.float32)
    b1a[:, :DI] = b1[:, :DI]
    b1g[:, :DI] = b1[:, DI:]
    w["W1a"] = W1a
    w["W1g"] = W1g
    w["b1a"] = np.ascontiguousarray(b1a.reshape(DEPTH, NHT, 128).transpose(0, 2, 1))
    w["b1g"] = np.ascontiguousarray(b1g.reshape(DEPTH, NHT, 128).transpose(0, 2, 1))
    W2p = np.zeros((DEPTH, DIP, D), dtype=np.float32)
    W2p[:, :DI, :] = W2
    w["W2p"] = W2p
    w["b2"] = np.ascontiguousarray(np.asarray(inputs["L0_ffn_b2"], np.float32))
    for k in ("wqkv_o", "wo_o", "wqkv_i", "wo_i", "W1a", "W1g", "W2p", "b2"):
        w[k] = np.ascontiguousarray(w[k].astype(BF))
    return w



class P:
    DMA_OWNER = {"sq": "sync", "gq": "gpsimd"}

    def __init__(self, nc):
        self.nc = nc
        self.q = {e: [] for e in ENGS}
        self.sems = {}
        self.cnt = {}
        self.blocks = 0
        self.dmat = {"sq": 0, "gq": 0}
        self.dmsw = {"sq": 0, "gq": 0}

    def sem(self, name):
        h = self.nc.alloc_semaphore(name)
        self.sems[name] = h
        self.cnt[name] = 0
        return h

    def _wait(self, eng, waits):
        for (s, v) in waits:
            if v <= 0:
                continue
            assert v <= self.cnt[s], f"wait {s}>={v} emitted={self.cnt[s]}"
            if s in self.dmat:
                if eng == self.DMA_OWNER[s]:
                    self.dmsw[s] = max(self.dmsw[s], v)
                else:
                    self.dmat[s] = max(self.dmat[s], v)
            h = self.sems[s]
            self.q[eng].append(lambda e, h=h, v=v: e.wait_ge(h, v))

    def dma_pin(self, s):
        if self.dmat[s] > self.dmsw[s]:
            v = self.dmat[s]
            h = self.sems[s]
            self.q[self.DMA_OWNER[s]].append(
                lambda e, h=h, v=v: e.wait_ge(h, v))
            self.dmsw[s] = v

    def do(self, eng, fn, waits=(), inc=None, incby=1):
        self._wait(eng, waits)
        if inc in self.dmat:
            self.dma_pin(inc)
        if inc is not None:
            self.cnt[inc] += incby
            h = self.sems[inc]
            if incby == 1 and eng in ("vector", "scalar", "gpsimd"):
                def _f(e, fn=fn, h=h, n=incby):
                    fn(e)
                    e.drain().then_inc(h, n)
                self.q[eng].append(_f)
            else:
                self.q[eng].append(
                    lambda e, fn=fn, h=h, n=incby: fn(e).then_inc(h, n))
        else:
            self.q[eng].append(fn)
        return self.cnt[inc] if inc is not None else None

    def raw(self, eng, fn):
        self.q[eng].append(fn)

    def val(self, name):
        return self.cnt[name]

    def bar(self):
        self.do("sync", lambda x: x.nop(), inc="ye")
        for e, s in (("tensor", "te"), ("vector", "ve"), ("scalar", "se"),
                     ("gpsimd", "ge")):
            h = self.sems[s]
            self.cnt[s] += 1
            self.q[e].append(lambda x, h=h: x.drain().then_inc(h, 1))
        vals = [(s, self.cnt[s]) for s in
                ("te", "ve", "se", "ge", "ye", "sq", "gq")]
        for e in ENGS:
            self._wait(e, vals)

    def flush(self):
        nq = {e: ops for e, ops in self.q.items() if ops}
        self.q = {e: [] for e in ENGS}
        if not nq:
            return
        self.blocks += 1
        with self.nc.Block(f"blk{self.blocks}") as blk:
            for e, ops in nq.items():
                getattr(blk, e)(lambda eng, ops=ops: [f(eng) for f in ops])



def build_program(debug=()):
    nc = bass.Bass("TRN2", target_bir_lowering=False)

    def din(name, shape, dt=F32):
        return nc.dram_tensor(name, list(shape), dt, kind="ExternalInput").ap()

    xfull = din("xfull", (N, D))
    xown = din("xown", (NOWN, D))
    wqkv_o = din("wqkv_o", (D, 3 * D), BF16)
    wo_o = din("wo_o", (D, D), BF16)
    logxi_r = din("logxi_r", (1, H))
    ebits_r = din("ebits_r", (1, H))
    ocos = din("ocos", (128, N))
    osin = din("osin", (128, N))
    ocos_q = din("ocos_q", (128, NOWN))
    osin_q = din("osin_q", (128, NOWN))
    ics = din("ics", (128, NOWN))
    iss = din("iss", (128, NOWN))
    ick = din("ick", (128, NOWN))
    isk = din("isk", (128, NOWN))
    permB = din("permB", (128, 128))
    ident = din("ident", (128, 128))
    identb = din("identb", (128, 128))
    onesf = din("onesf", (128, 512))
    onesb = din("onesb", (128, 512))
    ones2 = din("ones2", (128, 2))
    sel2 = din("sel2", (2, 128))
    posrel64 = din("posrel64", (128, QB))
    m1stair = din("m1stair", (128, 128))
    m2stairT = din("m2stairT", (128, 128))
    poskrel = din("poskrel", (NQB, 128, KT))
    omaskflag = din("omaskflag", (NQB, KT))
    ocausal = din("ocausal", (NQB, 8, 128, QB), BF16)
    w0prevmask = din("w0prevmask", (128, 512))
    mask01prev = din("mask01prev", (2, 2, 128, 256), BF16)
    mask01neg = din("mask01neg", (2, 2, 128, 256))
    padmask = din("padmask", (128, 1))
    slot_in = din("slot", (1, 1), U32)
    fvals = din("fvals", (1, 64), U32)
    wqkv_i = din("wqkv_i", (DEPTH, D, 3 * D), BF16)
    wo_i = din("wo_i", (DEPTH, D, D), BF16)
    qs2 = din("qs2", (DEPTH, 128, 1))
    ks2 = din("ks2", (DEPTH, 128, 1))
    lng = din("lng", (DEPTH, D))
    lnb = din("lnb", (DEPTH, D))
    W1a = din("W1a", (DEPTH, D, DIP), BF16)
    W1g = din("W1g", (DEPTH, D, DIP), BF16)
    b1a_i = din("b1a", (DEPTH, 128, NHT))
    b1g_i = din("b1g", (DEPTH, 128, NHT))
    W2p = din("W2p", (DEPTH, DIP, D), BF16)
    b2_i = din("b2", (DEPTH, D), BF16)

    y = nc.dram_tensor("y", [NOWN, D], F32, kind="ExternalOutput").ap()
    dbg = {}
    for nm, shape in debug:
        dbg[nm] = nc.dram_tensor("d_" + nm, list(shape), F32,
                                 kind="ExternalOutput").ap()

    kscr = nc.dram_tensor("kscr", [8, 128, N], BF16, kind="Internal").ap()
    qscr = nc.dram_tensor("qscr", [8, 128, NOWN], BF16, kind="Internal").ap()
    vscr_o = nc.dram_tensor("vscr_o", [KT, 128, HG], BF16, kind="Internal").ap()
    x1scr = nc.dram_tensor("x1scr", [8, 128, NOWN], F32, kind="Internal").ap()
    vscr_i = nc.dram_tensor("vscr_i", [2 + NOWN // 128, 128, HG], BF16,
                            kind="Internal").ap()
    kscr_i = nc.dram_tensor("kscr_i", [8, 128, NOWN], BF16,
                            kind="Internal").ap()
    qscr_i = nc.dram_tensor("qscr_i", [8, 128, NOWN], BF16,
                            kind="Internal").ap()
    mbsp = {} if MB_SELF else {"addr_space": "Shared"}
    mb_x = nc.dram_tensor("mb_x", [2, 2, 128, 8], F32, kind="Internal",
                          **mbsp).ap()
    mb_k = nc.dram_tensor("mb_k", [2, 2, 8, 128, WIN], BF16, kind="Internal",
                          **mbsp).ap()
    mb_v = nc.dram_tensor("mb_v", [2, 2, 2, 128, HG], BF16, kind="Internal",
                          **mbsp).ap()
    mb_fl = nc.dram_tensor("mb_fl", [2, 64], U32, kind="Internal",
                           **mbsp).ap()

    p = P(nc)
    for s in ("sq", "gq", "te", "ve", "se", "ge", "ye", "pl"):
        p.sem(s)

    st = contextlib.ExitStack()
    sb = lambda name, shape, dt: st.enter_context(
        nc.sbuf_tensor(name, list(shape), dt))
    PS = [st.enter_context(nc.psum_tensor(f"ps{i}", [128, 512], F32))
          for i in range(8)]

    xres = [sb(f"xres{i}", (128, 1 + NOWN), F32) for i in range(8)]
    t_onesf = sb("c_onesf", (128, 512), F32)
    t_onesb = sb("c_onesb", (128, 512), BF16)
    t_ident = sb("c_ident", (128, 128), F32)
    t_identb = sb("c_identb", (128, 128), BF16)
    t_perm = sb("c_perm", (128, 128), BF16)
    t_ones2 = sb("c_ones2", (128, 2), BF16)
    t_sel2 = sb("c_sel2", (2, 128), F32)
    t_pad = sb("c_pad", (128, 1), F32)
    t_slot = sb("c_slot", (1, 1), U32)
    t_fv = sb("c_fv", (1, 64), U32)
    t_stage = sb("c_stage", (128, 1024), F32)
    t_stageb = sb("c_stageb", (128, 1024), BF16)
    t_m1 = sb("c_m1", (128, 128), F32)
    t_m2T = sb("c_m2T", (128, 128), F32)
    t_w0pm = sb("c_w0pm", (128, 512), F32)
    t_m01n = sb("c_m01n", (128, 2 * 2 * 256), F32)
    t_flagr = sb("c_flagr", (1, 64), U32)
    t_xcol = sb("c_xcol", (128, 8), F32)
    t_xcolr = sb("c_xcolr", (128, 8), F32)
    t_cast = sb("t_cast", (128, 512), BF16)
    t_cast2 = sb("t_cast2", (128, 512), BF16)
    t_mu = sb("t_mu", (1, 512), F32)
    t_rstd = sb("t_rstd", (1, 512), F32)
    t_tmp1 = sb("t_tmp1", (1, 512), F32)
    st2 = contextlib.ExitStack()
    sb2 = lambda name, shape, dt: st2.enter_context(
        nc.sbuf_tensor(name, list(shape), dt))
    t_c8 = sb2("c_c8", (128, H), F32)
    t_negc = sb2("c_negc", (128, H), F32)
    t_btab = sb2("c_btab", (128, NQB * H * KT), F32)
    t_posr64 = sb2("c_posr64", (128, QB), F32)

    for dst, src in ((t_onesf[:], onesf), (t_ident[:], ident),
                     (t_m1[:], m1stair), (t_m2T[:], m2stairT),
                     (t_w0pm[:], w0prevmask), (t_pad[:], padmask),
                     (t_slot[:], slot_in), (t_fv[:], fvals),
                     (t_sel2[:], sel2), (t_posr64[:], posrel64)):
        p.do("sync", lambda e, d=dst, s=src: e.dma_start(out=d, in_=s),
             inc="sq", incby=16)
    for dst, src in ((t_onesb[:], onesb), (t_perm[:], permB),
                     (t_ones2[:], ones2), (t_identb[:], identb)):
        p.do("gpsimd", lambda e, d=dst, s=src: e.dma_start(out=d, in_=s),
             inc="gq", incby=16)
    p.do("gpsimd", lambda e: e.dma_start(
        out=t_m01n.rearrange("p (a b f) -> p a b f", a=2, b=2),
        in_=mask01neg.rearrange("a b p f -> p a b f")), inc="gq", incby=16)
    p.bar()

    mbx = {"n": 0}

    def _mb_init(e):
        mbx["myr"] = e.alloc_register("myslotr")
        e.reg_load(mbx["myr"], t_slot[0:1, 0:1])
        mbx["tgt"] = e.alloc_register("polltgt")
        e.reg_mov(mbx["tgt"], 0)
    if STOP_AFTER != "consts_nomb":
        p.raw("sync", _mb_init)
    if STOP_AFTER in ("consts", "consts_nomb"):
        yv0 = y.rearrange("t d -> (t d)").rearrange("(a c) -> a c", a=1024)
        for pt in range(8):
            p.do("sync", lambda e, pt=pt: e.dma_start(
                out=yv0[pt * 128:(pt + 1) * 128, :],
                in_=xres[pt][:, 1:1 + NOWN]), inc="sq", incby=16)
        p.bar()
        p.flush()
        st2.close()
        st.close()
        return nc

    def mb_dual(ap0_fn, ap1_fn, inc=None, incby=16, waits=(), eng="sync"):
        p._wait(eng, waits)
        if inc in p.dmat:
            p.dma_pin(inc)
        if MB_SELF:
            if inc is not None:
                base = p.cnt[inc]
                p.cnt[inc] += 2 * incby
                h = p.sems[inc]

                def _f2(e, a0=ap0_fn, a1=ap1_fn, h=h, n=incby, base=base):
                    a0(e).then_inc(h, n)
                    e.wait_ge(h, base + n)
                    a1(e).then_inc(h, n)
                p.q[eng].append(_f2)
                return p.cnt[inc]

            def _g2(e, a0=ap0_fn, a1=ap1_fn):
                a0(e)
                a1(e)
            p.q[eng].append(_g2)
            return None
        if inc is not None:
            p.cnt[inc] += incby
            h = p.sems[inc]
            def _f(e, a0=ap0_fn, a1=ap1_fn, h=h, n=incby):
                with e.If_eq(mbx["myr"], 0):
                    a0(e).then_inc(h, n)
                with e.Else():
                    a1(e).then_inc(h, n)
            p.q[eng].append(_f)
            return p.cnt[inc]
        def _g(e, a0=ap0_fn, a1=ap1_fn):
            with e.If_eq(mbx["myr"], 0):
                a0(e)
            with e.Else():
                a1(e)
        p.q[eng].append(_g)


    def dprobe(tag):
        pass

    def mb_flag(waits=()):
        k = mbx["n"]
        mbx["n"] += 1
        assert k < 64
        mb_dual(lambda e, k=k: e.dma_start(out=mb_fl[0, k:k + 1],
                                           in_=t_fv[0:1, k:k + 1]),
                lambda e, k=k: e.dma_start(out=mb_fl[1, k:k + 1],
                                           in_=t_fv[0:1, k:k + 1]),
                inc="sq", waits=waits)
        return k

    def mb_poll(k):
        sqv = p.val("sq")

        def _poll(e):
            if MB_SELF:
                e.wait_ge(p.sems["sq"], sqv)
            with e.register(f"c{k}") as cond, e.register(f"f{k}") as fvr, \
                 e.register(f"x{k}") as expr:
                e.reg_load(expr, t_fv[0:1, k:k + 1])
                e.reg_mov(cond, 1)
                with e.While(cond):
                    e.reg_alu(mbx["tgt"], mbx["tgt"], 16, ALU.add)
                    with e.If_eq(mbx["myr"], 0):
                        e.dma_start(out=t_flagr[0:1, k:k + 1],
                                    in_=mb_fl[1, k:k + 1]
                                    ).then_inc(p.sems["pl"], 16)
                    with e.Else():
                        e.dma_start(out=t_flagr[0:1, k:k + 1],
                                    in_=mb_fl[0, k:k + 1]
                                    ).then_inc(p.sems["pl"], 16)
                    e.wait_ge(p.sems["pl"], mbx["tgt"])
                    e.reg_load(fvr, t_flagr[0:1, k:k + 1])
                    e.reg_alu(cond, fvr, expr, ALU.not_equal)
        p.raw("sync", _poll)

    def dbg_dump(name, pairs):
        if name not in dbg:
            return
        for sap, dap in pairs:
            p.do("sync", lambda e, s=sap, d=dap: e.dma_start(out=d, in_=s),
                 inc="sq", incby=16)
        p.bar()

    class _SkipRest(Exception):
        pass

    def dump_stop_early():
        yv0 = y.rearrange("t d -> (t d)").rearrange("(a c) -> a c", a=1024)
        p.bar()
        if not SIM_SKIP_UNINIT_DUMP:
            for pt in range(8):
                p.do("sync", lambda e, pt=pt: e.dma_start(
                    out=yv0[pt * 128:(pt + 1) * 128, :],
                    in_=xres[pt][:, 1:1 + NOWN]), inc="sq", incby=16)
            p.bar()
        p.flush()
        st2.close()
        st.close()
        return nc

    with nc.sbuf_tensor("t_lxr", [1, H], F32) as t_lxr, \
         nc.sbuf_tensor("t_ebr", [1, H], F32) as t_ebr, \
         nc.sbuf_tensor("t_crw", [1, H], F32) as t_crw, \
         nc.sbuf_tensor("t_pk", [128, NQB * KT], F32) as t_pk, \
         nc.sbuf_tensor("t_omf", [1, NQB * KT], F32) as t_omf, \
         nc.sbuf_tensor("t_omfb", [128, NQB * KT], F32) as t_omfb:
        for d_, s_ in ((t_lxr[:], logxi_r), (t_ebr[:], ebits_r),
                       (t_omf[:], omaskflag.rearrange("a b -> (a b)").rearrange("(o c) -> o c", o=1))):
            p.do("sync", lambda e, d=d_, s=s_: e.dma_start(out=d, in_=s),
                 inc="sq", incby=16)
        p.do("sync", lambda e: e.dma_start(
            out=t_pk.rearrange("p (a b) -> p a b", a=NQB),
            in_=poskrel.rearrange("a p b -> p a b")), inc="sq", incby=16)
        p.bar()
        _s = STOP_AFTER or ""
        if _s.startswith("p0b4"):
            stoplv = 4
        elif _s in ("p0b1", "p0b2", "p0b3"):
            stoplv = int(_s[-1])
        else:
            stoplv = {"p0a": 0, "p0b": 5, "p0c": 6}.get(_s, 9)
        if stoplv >= 1:
            p.do("scalar", lambda e: e.activation(out=t_crw[:], in_=t_lxr[:],
                                                  func=AF.Exp, scale=-1.0),
                 inc="se")
            if stoplv >= 2:
                p.do("vector", lambda e: e.tensor_mul(t_crw[:], t_crw[:],
                                                      t_ebr[:]),
                     waits=(("se", p.val("se")),), inc="ve")
            if stoplv >= 3:
                p.do("tensor", lambda e: e.matmul(PS[0][:, 0:H],
                                                  t_onesf[0:1, 0:128],
                                                  t_crw[:], start=True,
                                                  stop=True),
                     waits=(("ve", p.val("ve")),), inc="te")
            if stoplv >= 4:
                b4v = _s[4:] if _s.startswith("p0b4") else ""
                if b4v == "a":
                    p.do("scalar", lambda e: e.mul(t_c8[:], t_onesf[:, 0:H],
                                                   8.0),
                         waits=(("te", p.val("te")),), inc="se")
                elif b4v == "b":
                    p._wait("scalar", (("te", p.val("te")),))
                    p.cnt["se"] += 1
                    p.raw("scalar", lambda e: e.mul(
                        t_c8[:], PS[0][:, 0:H], 8.0).then_inc(
                            p.sems["se"], 1))
                elif b4v == "c":
                    p.do("vector", lambda e: e.tensor_scalar(
                        out=t_c8[:], in0=PS[0][:, 0:H], scalar1=8.0,
                        scalar2=None, op0=ALU.mult),
                        waits=(("te", p.val("te")),), inc="ve")
                elif b4v == "e":
                    p.do("scalar", lambda e: e.activation(
                        out=t_c8[:], in_=PS[0][:, 0:H], func=AF.Copy,
                        scale=8.0),
                        waits=(("te", p.val("te")),), inc="se")
                else:
                    p.do("scalar", lambda e: e.mul(t_c8[:], PS[0][:, 0:H],
                                                   8.0),
                         waits=(("te", p.val("te")),), inc="se")
            if stoplv >= 5:
                p.do("scalar", lambda e: e.mul(t_negc[:], PS[0][:, 0:H],
                                               -1.0), inc="se")
            p.bar()
        if stoplv >= 6:
            for qb in range(NQB):
                p.do("tensor", lambda e, qb=qb: e.matmul(
                    PS[1][:, qb * KT:(qb + 1) * KT], t_onesf[0:1, 0:128],
                    t_omf[0:1, qb * KT:(qb + 1) * KT],
                    start=(qb == 0), stop=(qb == NQB - 1)),
                    inc="te" if qb == NQB - 1 else None)
            p.do("scalar", lambda e: e.copy(t_omfb[:], PS[1][:, 0:NQB * KT]),
                 waits=(("te", p.val("te")),), inc="se")
        if stoplv >= 7:
            s1 = p.val("se")
            for qb in range(NQB):
                for h in range(H):
                    c0 = ((qb * H) + h) * KT
                    p.do("vector", lambda e, qb=qb, h=h, c0=c0:
                         e.scalar_tensor_tensor(
                             out=t_btab[:, c0:c0 + KT],
                             in0=t_pk[:, qb * KT:(qb + 1) * KT],
                             scalar=t_negc[:, h:h + 1],
                             in1=t_omfb[:, qb * KT:(qb + 1) * KT],
                             op0=ALU.mult, op1=ALU.add),
                         waits=(("se", s1),) if (qb == 0 and h == 0) else ())
        p.bar()
        p.flush()
    if stoplv <= 6:
        return dump_stop_early()

    if STOP_AFTER == "p0":
        yv0 = y.rearrange("t d -> (t d)").rearrange("(a c) -> a c", a=1024)
        p.bar()
        if not SIM_SKIP_UNINIT_DUMP:
            for pt in range(8):
                p.do("sync", lambda e, pt=pt: e.dma_start(
                    out=yv0[pt * 128:(pt + 1) * 128, :],
                    in_=xres[pt][:, 1:1 + NOWN]), inc="sq", incby=16)
            p.bar()
        p.flush()
        st2.close()
        st.close()
        return nc

    def emit_ln_stats(srcs):
        for pt in range(8):
            sl = srcs[pt]
            w = [("te", p.val("te"))] if pt > 0 else ()
            p.do("scalar", lambda e, sl=sl: e.copy(t_cast[:], sl),
                 waits=w, inc="se")
            p.do("scalar", lambda e, sl=sl: e.activation(
                out=t_cast2[:], in_=sl, func=AF.Square), inc="se")
            sv = p.val("se")
            p.do("tensor", lambda e, pt=pt: e.matmul(
                PS[2][0:1, 0:512], t_onesb[:, 0:1], t_cast[:],
                start=(pt == 0), stop=(pt == 7)), waits=(("se", sv),))
            p.do("tensor", lambda e, pt=pt: e.matmul(
                PS[3][0:1, 0:512], t_onesb[:, 0:1], t_cast2[:],
                start=(pt == 0), stop=(pt == 7)), inc="te")
        tw = p.val("te")
        p.do("scalar", lambda e: e.mul(t_mu[:], PS[2][0:1, 0:512], 1.0 / D),
             waits=(("te", tw),), inc="se")
        p.do("scalar", lambda e: e.mul(t_tmp1[:], PS[3][0:1, 0:512], 1.0 / D),
             inc="se")
        sv = p.val("se")
        p.do("vector", lambda e: e.tensor_mul(t_rstd[:], t_mu[:], t_mu[:]),
             waits=(("se", sv),), inc="ve")
        p.do("vector", lambda e: e.scalar_tensor_tensor(
            out=t_tmp1[:], in0=t_tmp1[:], scalar=1e-5, in1=t_rstd[:],
            op0=ALU.add, op1=ALU.subtract),
            waits=(("ve", p.val("ve")),), inc="ve")
        p.do("scalar", lambda e: e.activation(out=t_tmp1[:], in_=t_tmp1[:],
                                              func=AF.Sqrt),
             waits=(("ve", p.val("ve")),), inc="se")
        p.do("vector", lambda e: e.reciprocal(t_rstd[:], t_tmp1[:]),
             waits=(("se", p.val("se")),), inc="ve")
        p.do("vector", lambda e: e.scalar_tensor_tensor(
            out=t_mu[:], in0=t_mu[:], scalar=-1.0, in1=t_rstd[:],
            op0=ALU.mult, op1=ALU.mult),
            waits=(("ve", p.val("ve")),), inc="ve")
        p.bar()

    dprobe('postP0')
    with nc.sbuf_tensor("wqb", [128, 8 * 3072], BF16) as wqb, \
         nc.sbuf_tensor("xtm", [128, 4 * 1024], F32) as xtm, \
         nc.sbuf_tensor("xbb", [128, 8 * 512], BF16) as xbb, \
         nc.sbuf_tensor("tcs", [128, 512], F32) as tcs, \
         nc.sbuf_tensor("tsn", [128, 512], F32) as tsn, \
         nc.sbuf_tensor("tf_", [128, 512], F32) as tf_, \
         nc.sbuf_tensor("tb_", [128, 512], BF16) as tb_, \
         nc.sbuf_tensor("tm1_", [128, 512], F32) as tm1_, \
         nc.sbuf_tensor("tm2_", [128, 512], F32) as tm2_, \
         nc.sbuf_tensor("kout", [128, 512], BF16) as kout, \
         nc.sbuf_tensor("vblk", [128, HG], BF16) as vblk:

        for kt in range(8):
            p.do("gpsimd", lambda e, kt=kt: e.dma_start(
                out=wqb[:, kt * 3072:(kt + 1) * 3072],
                in_=wqkv_o[kt * 128:(kt + 1) * 128, :]), inc="gq", incby=16)
        p.bar()

        def xpose_block(src, blk, fill_xres):
            for tt in range(4):
                p.do("sync", lambda e, tt=tt, blk=blk, src=src: e.dma_start(
                    out=xtm[:, tt * 1024:(tt + 1) * 1024],
                    in_=src[blk * 512 + tt * 128: blk * 512 + (tt + 1) * 128, :]),
                    inc="sq", incby=16)
            lw = p.val("sq")
            for i, (tt, pt) in enumerate(
                    (t, q) for t in range(4) for q in range(8)):
                pb = PS[i % 2]
                w = [("sq", lw)] if i == 0 else [("se", p.val("se") - 1)]
                p.do("tensor", lambda e, tt=tt, pt=pt, pb=pb: e.transpose(
                    pb[:, 0:128],
                    xtm[:, tt * 1024 + pt * 128: tt * 1024 + (pt + 1) * 128],
                    t_ident[:]), waits=w, inc="te")
                tw = p.val("te")
                p.do("scalar", lambda e, tt=tt, pt=pt, pb=pb: e.copy(
                    xbb[:, pt * 512 + tt * 128: pt * 512 + (tt + 1) * 128],
                    pb[:, 0:128]), waits=(("te", tw),), inc="se")
                if fill_xres:
                    p.do("scalar",
                         lambda e, tt=tt, pt=pt, pb=pb, blk=blk: e.copy(
                             xres[pt][:, 1 + blk * 512 + tt * 128:
                                      1 + blk * 512 + (tt + 1) * 128],
                             pb[:, 0:128]), inc="se")
            p.bar()

        def rope_pair(psum, dst):
            p.do("scalar", lambda e, psum=psum: e.copy(tf_[:], psum[:, 0:512]),
                 inc="se")
            p.do("scalar", lambda e, psum=psum: e.copy(tb_[:], psum[:, 0:512]),
                 inc="se")
            sw = p.val("se")
            p.do("tensor", lambda e: e.matmul(PS[6][:, 0:512], t_perm[:],
                                              tb_[:], start=True, stop=True),
                 waits=(("se", sw),), inc="te")
            tpm = p.val("te")
            p.do("vector", lambda e: e.tensor_mul(tm1_[:], tf_[:], tcs[:]),
                 waits=(("se", sw),))
            p.do("vector", lambda e: e.tensor_mul(tm2_[:], tsn[:],
                                                  PS[6][:, 0:512]),
                 waits=(("te", tpm),), inc="ve")
            p.do("vector", lambda e, dst=dst: e.tensor_add(dst, tm1_[:],
                                                           tm2_[:]),
                 waits=(("ve", p.val("ve")),), inc="ve")

        def qk_pass(src_x, wcol0, cost, sint, dst_scr, fill_xres, nblk):
            for blk in range(nblk):
                xpose_block(src_x, blk, fill_xres)
                p.do("sync", lambda e, blk=blk, cost=cost: e.dma_start(
                    out=tcs[:], in_=cost[:, blk * 512:(blk + 1) * 512]),
                    inc="sq", incby=16)
                p.do("sync", lambda e, blk=blk, sint=sint: e.dma_start(
                    out=tsn[:], in_=sint[:, blk * 512:(blk + 1) * 512]),
                    inc="sq", incby=16)
                p.bar()
                for m in range(8):
                    pk = PS[2 + (m % 2)]
                    for kt in range(8):
                        p.do("tensor", lambda e, m=m, kt=kt, pk=pk,
                             wcol0=wcol0: e.matmul(
                                 pk[:, 0:512],
                                 wqb[:, kt * 3072 + wcol0 + m * 128:
                                     kt * 3072 + wcol0 + (m + 1) * 128],
                                 xbb[:, kt * 512:(kt + 1) * 512],
                                 start=(kt == 0), stop=(kt == 7)),
                             inc="te" if kt == 7 else None)
                    tmm = p.val("te")
                    if m % 2 == 1:
                        p._wait("scalar", (("te", tmm),))
                        rope_pair(pk, kout[:])
                        vv = p.val("ve")
                        p.do("sync", lambda e, m=m, blk=blk,
                             dst_scr=dst_scr: e.dma_start(
                                 out=dst_scr[m, :, blk * 512:(blk + 1) * 512],
                                 in_=kout[:]),
                             waits=(("ve", vv),), inc="sq", incby=16)
                    else:
                        p.do("scalar", lambda e, pk=pk: e.copy(
                            kout[:], pk[:, 0:512]),
                            waits=(("te", tmm),), inc="se")
                        sv = p.val("se")
                        p.do("sync", lambda e, m=m, blk=blk,
                             dst_scr=dst_scr: e.dma_start(
                                 out=dst_scr[m, :, blk * 512:(blk + 1) * 512],
                                 in_=kout[:]),
                             waits=(("se", sv),), inc="sq", incby=16)
                    p.bar()
                yield blk

        for blk in qk_pass(xfull, 1024, ocos, osin, kscr, False, 8):
            for tt in range(4):
                for kt in range(8):
                    p.do("tensor", lambda e, tt=tt, kt=kt: e.matmul(
                        PS[4][:, 0:512],
                        xbb[:, kt * 512 + tt * 128: kt * 512 + (tt + 1) * 128],
                        wqb[:, kt * 3072 + 2048: kt * 3072 + 2560],
                        start=(kt == 0), stop=(kt == 7)))
                    p.do("tensor", lambda e, tt=tt, kt=kt: e.matmul(
                        PS[5][:, 0:512],
                        xbb[:, kt * 512 + tt * 128: kt * 512 + (tt + 1) * 128],
                        wqb[:, kt * 3072 + 2560: kt * 3072 + 3072],
                        start=(kt == 0), stop=(kt == 7)),
                        inc="te" if kt == 7 else None)
                tv = p.val("te")
                p.do("vector", lambda e: e.memset(
                    vblk.rearrange("p (h c) -> p h c", h=H)[:, :, 64:65], 1.0),
                    waits=(("te", tv),), inc="ve")
                vv = p.val("ve")
                for half, pv in ((0, PS[4]), (1, PS[5])):
                    p.do("scalar", lambda e, half=half, pv=pv: e.copy(
                        vblk.rearrange("p (h c) -> p h c", h=H)[
                            :, half * 8:(half + 1) * 8, 0:64],
                        pv[:, 0:512].rearrange("p (h c) -> p h c", h=8)),
                        waits=(("ve", vv),), inc="se")
                sv = p.val("se")
                p.do("sync", lambda e, blk=blk, tt=tt: e.dma_start(
                    out=vscr_o[blk * 4 + tt], in_=vblk[:]),
                    waits=(("se", sv),), inc="sq", incby=16)
                p.bar()
        for blk in qk_pass(xown, 0, ocos_q, osin_q, qscr, True, 4):
            pass
        p.flush()

    dbg_dump("x0res", [(xres[pt][:, 1:1 + NOWN],
                        dbg["x0res"][pt * 128:(pt + 1) * 128, :])
                       for pt in range(8)] if "x0res" in dbg else [])

    dprobe('postP1')
    yv = y.rearrange("t d -> (t d)").rearrange("(a c) -> a c", a=1024)

    def dump_xres_and_stop():
        p.bar()
        for pt in range(8):
            p.do("sync", lambda e, pt=pt: e.dma_start(
                out=yv[pt * 128:(pt + 1) * 128, :],
                in_=xres[pt][:, 1:1 + NOWN]), inc="sq", incby=16)
        p.bar()
        p.flush()
        try:
            st2.close()
        except Exception:
            pass
        st.close()
        return nc

    if STOP_AFTER == "p1":
        return dump_xres_and_stop()

    with nc.sbuf_tensor("t_qqb", [128, 8 * 512], BF16) as t_qqb, \
         nc.sbuf_tensor("t_kp", [128, KT * 128], BF16) as t_kp, \
         nc.sbuf_tensor("t_vp", [128, KT * 130], BF16) as t_vp, \
         nc.sbuf_tensor("t_oc", [128, 8 * 512], BF16) as t_oc, \
         nc.sbuf_tensor("t_pt0", [128, 512], BF16) as t_pt0, \
         nc.sbuf_tensor("t_pt1", [128, 512], BF16) as t_pt1, \
         nc.sbuf_tensor("t_rec", [128, 512], F32) as t_rec, \
         nc.sbuf_tensor("t_obuf", [128, 8 * 512], BF16) as t_obuf, \
         nc.sbuf_tensor("t_wob", [128, 8 * 1024], BF16) as t_wob, \
         nc.sbuf_tensor("t_ln", [128, 8 * 512], F32) as t_ln:

        t_pts = (t_pt0, t_pt1)
        for kt in range(8):
            p.do("gpsimd", lambda e, kt=kt: e.dma_start(
                out=t_wob[:, kt * 1024:(kt + 1) * 1024],
                in_=wo_o[kt * 128:(kt + 1) * 128, :]), inc="gq", incby=16)
        p.bar()
        for qb in range(NQB):
            TB = 20 + 4 * qb
            stair_slots = {(4 * qb + m): m for m in range(4)}
            stair_slots.update({(16 + 4 * qb + m): 4 + m for m in range(4)})
            p.do("sync", lambda e, qb=qb: e.dma_start(
                out=t_qqb.rearrange("p (a c) -> p a c", a=8),
                in_=qscr[:, :, qb * QB:(qb + 1) * QB].rearrange(
                    "a p c -> p a c")), inc="sq", incby=16)
            p.do("sync", lambda e, qb=qb: e.dma_start(
                out=t_oc.rearrange("p (a c) -> p a c", a=8),
                in_=ocausal[qb].rearrange("a p c -> p a c")),
                inc="sq", incby=16)
            p.bar()
            for hp in range(8):
                p.do("sync", lambda e, hp=hp: e.dma_start(
                    out=t_kp[:], in_=kscr[hp]), inc="sq", incby=16)
                p.do("gpsimd", lambda e, hp=hp: e.dma_start(
                    out=t_vp.rearrange("p (a c) -> p a c", a=KT, c=130),
                    in_=vscr_o[:, :, hp * 130:(hp + 1) * 130].rearrange(
                        "a p c -> p a c")), inc="gq", incby=16)
                p.bar()
                for hh in range(2):
                    h = 2 * hp + hh
                    p.do("vector", lambda e, h=h, hh=hh: e.tensor_scalar(
                        out=t_stage[64:65, hh * 512:(hh + 1) * 512],
                        in0=t_posr64[64:65, 0:QB],
                        scalar1=t_c8[64:65, h:h + 1], scalar2=None,
                        op0=ALU.mult), inc="ve")
                    stg = p.val("ve")
                    p._wait("tensor", (("ve", stg),))
                    expdone, avdone = {}, {}
                    for kt in range(TB):
                        psS = PS[kt % 2]
                        w = [("se", expdone[kt - 2])] if kt >= 2 else []
                        p.do("tensor", lambda e, kt=kt, hh=hh, hp=hp, psS=psS:
                             e.matmul(psS[:, 0:512],
                                      t_kp[hh * 64:(hh + 1) * 64,
                                           kt * 128:(kt + 1) * 128],
                                      t_qqb[hh * 64:(hh + 1) * 64,
                                            hp * 512:(hp + 1) * 512],
                                      start=True, stop=False), waits=w)
                        p.do("tensor", lambda e, hh=hh, psS=psS: e.matmul(
                            psS[:, 0:512], t_onesf[64:65, 0:128],
                            t_stage[64:65, hh * 512:(hh + 1) * 512],
                            start=False, stop=True), inc="te")
                        tmm = p.val("te")
                        if kt in stair_slots:
                            s = stair_slots[kt]
                            p.do("vector", lambda e, s=s, psS=psS: e.tensor_add(
                                psS[:, 0:512], psS[:, 0:512],
                                t_oc[:, s * 512:(s + 1) * 512]),
                                waits=(("te", tmm),), inc="ve")
                            mw = [("ve", p.val("ve"))]
                        else:
                            mw = [("te", tmm)]
                        if kt >= 2:
                            mw.append(("te", avdone[kt - 2]))
                        bcol = ((qb * H) + h) * KT + kt
                        p.do("scalar", lambda e, kt=kt, bcol=bcol, psS=psS:
                             e.activation(out=t_pts[kt % 2][:],
                                          in_=psS[:, 0:512], func=AF.Exp,
                                          scale=0.125,
                                          bias=t_btab[:, bcol:bcol + 1]),
                             waits=mw, inc="se")
                        expdone[kt] = p.val("se")
                        p.do("tensor", lambda e, kt=kt, hh=hh, TB=TB: e.matmul(
                            PS[4][0:65, 0:512],
                            t_vp[:, kt * 130 + hh * 65:
                                 kt * 130 + (hh + 1) * 65],
                            t_pts[kt % 2][:],
                            start=(kt == 0), stop=(kt == TB - 1)),
                            waits=(("se", expdone[kt]),), inc="te")
                        avdone[kt] = p.val("te")
                    tw = p.val("te")

                    def _recip_o(e):
                        with nc.allow_low_precision(
                                reason="recip feeds bf16 matmul; same "
                                       "precision as prior copy-cast"):
                            e.reciprocal(t_stageb[64:65, 0:512],
                                         PS[4][64:65, 0:512])
                    p.do("vector", _recip_o, waits=(("te", tw),), inc="ve")
                    vv = p.val("ve")
                    p.do("tensor", lambda e: e.matmul(
                        PS[6][0:64, 0:512], t_onesb[64:65, 0:64],
                        t_stageb[64:65, 0:512], start=True, stop=True),
                        waits=(("ve", vv), ("se", p.val("se"))), inc="te")
                    tw2 = p.val("te")
                    p.do("scalar", lambda e: e.copy(t_rec[0:64, 0:512],
                                                    PS[6][0:64, 0:512]),
                         waits=(("te", tw2),), inc="se")
                    sv = p.val("se")
                    p.do("vector", lambda e, hp=hp, hh=hh: e.tensor_mul(
                        t_obuf[hh * 64:(hh + 1) * 64,
                               hp * 512:(hp + 1) * 512],
                        PS[4][0:64, 0:512], t_rec[0:64, 0:512]),
                        waits=(("se", sv),), inc="ve")
                    p.bar()
            for grp in range(2):
                for mi in range(4):
                    mt = grp * 4 + mi
                    for ktp in range(8):
                        p.do("tensor", lambda e, mt=mt, ktp=ktp, mi=mi:
                             e.matmul(PS[4 + mi][:, 0:512],
                                      t_wob[:, ktp * 1024 + mt * 128:
                                            ktp * 1024 + (mt + 1) * 128],
                                      t_obuf[:, ktp * 512:(ktp + 1) * 512],
                                      start=(ktp == 0), stop=(ktp == 7)),
                             inc="te" if ktp == 7 else None)
                tw = p.val("te")
                for mi in range(4):
                    mt = grp * 4 + mi
                    p.do("vector", lambda e, mt=mt, mi=mi, qb=qb: e.tensor_add(
                        t_ln[:, mt * 512:(mt + 1) * 512],
                        xres[mt][:, 1 + qb * QB: 1 + (qb + 1) * QB],
                        PS[4 + mi][:, 0:512]),
                        waits=(("te", tw),), inc="ve")
                p.bar()
            emit_ln_stats([t_ln[:, pt * 512:(pt + 1) * 512]
                           for pt in range(8)])
            p.do("tensor", lambda e: e.matmul(
                PS[2][:, 0:512], t_onesf[0:1, 0:128], t_rstd[:],
                start=True, stop=True))
            p.do("tensor", lambda e: e.matmul(
                PS[3][:, 0:512], t_onesf[0:1, 0:128], t_mu[:],
                start=True, stop=True), inc="te")
            tw = p.val("te")
            for pt in range(8):
                p.do("vector", lambda e, pt=pt: e.tensor_mul(
                    t_ln[:, pt * 512:(pt + 1) * 512],
                    t_ln[:, pt * 512:(pt + 1) * 512], PS[2][:, 0:512]),
                    waits=(("te", tw),) if pt == 0 else (),
                    inc="ve" if pt == 7 else None)
            mv = p.val("ve")
            for pt in range(8):
                p.do("vector", lambda e, pt=pt, qb=qb: e.tensor_add(
                    xres[pt][:, 1 + qb * QB: 1 + (qb + 1) * QB],
                    t_ln[:, pt * 512:(pt + 1) * 512], PS[3][:, 0:512]),
                    waits=(("ve", mv),) if pt == 0 else (), inc="ve")
            vv = p.val("ve")
            for pt in range(8):
                p.do("sync", lambda e, pt=pt, qb=qb: e.dma_start(
                    out=x1scr[pt, :, qb * QB:(qb + 1) * QB],
                    in_=xres[pt][:, 1 + qb * QB: 1 + (qb + 1) * QB]),
                    waits=(("ve", vv),) if pt == 0 else (),
                    inc="sq", incby=16)
            p.bar()
        p.flush()

    dbg_dump("x1", [(xres[pt][:, 1:1 + NOWN],
                     dbg["x1"][pt * 128:(pt + 1) * 128, :])
                    for pt in range(8)] if "x1" in dbg else [])

    if STOP_AFTER == "p2":
        return dump_xres_and_stop()

    st2.close()
    t_qs = sb("t_qs", (128, 1), F32)
    t_ks = sb("t_ks", (128, 1), F32)

    def publish_xcol(par, ev_waits):
        for pt in range(8):
            p.do("scalar", lambda e, pt=pt: e.copy(
                t_xcol[:, pt:pt + 1], xres[pt][:, NOWN:NOWN + 1]),
                waits=ev_waits if pt == 0 else (), inc="se")
        sv = p.val("se")
        mb_dual(lambda e, par=par: e.dma_start(out=mb_x[0, par],
                                               in_=t_xcol[:]),
                lambda e, par=par: e.dma_start(out=mb_x[1, par],
                                               in_=t_xcol[:]),
                inc="sq", waits=(("se", sv),))
        return mb_flag(waits=(("sq", p.val("sq")),))

    def recv_xcol(k, par):
        mb_poll(k)
        mb_dual(lambda e, par=par: e.dma_start(out=t_xcolr[:],
                                               in_=mb_x[1, par]),
                lambda e, par=par: e.dma_start(out=t_xcolr[:],
                                               in_=mb_x[0, par]),
                inc="sq")
        sv = p.val("sq")
        p.do("vector", lambda e: e.tensor_scalar(
            out=t_xcolr[:], in0=t_xcolr[:], scalar1=t_pad[:, 0:1],
            scalar2=None, op0=ALU.mult), waits=(("sq", sv),), inc="ve")
        vv = p.val("ve")
        for pt in range(8):
            p.do("scalar", lambda e, pt=pt: e.copy(
                xres[pt][:, 0:1], t_xcolr[:, pt:pt + 1]),
                waits=(("ve", vv),) if pt == 0 else (), inc="se")
        p.bar()

    for L in range(DEPTH):
        par = L % 2
        dprobe('L%d-start' % L)
        k_ev = publish_xcol(par, ())
        recv_xcol(k_ev, par)
        p.do("sync", lambda e, L=L: e.dma_start(out=t_qs[:], in_=qs2[L]),
             inc="sq", incby=16)
        p.do("sync", lambda e, L=L: e.dma_start(out=t_ks[:], in_=ks2[L]),
             inc="sq", incby=16)
        p.bar()

        with nc.sbuf_tensor(f"wqi{L}", [128, 8 * 1024], BF16) as wqi, \
             nc.sbuf_tensor(f"xsb{L}", [128, 8 * 1025], BF16) as xsb, \
             nc.sbuf_tensor(f"tcsi{L}", [128, 1024], F32) as tcsi, \
             nc.sbuf_tensor(f"tsni{L}", [128, 1024], F32) as tsni, \
             nc.sbuf_tensor(f"qko{L}", [128, 512], BF16) as qko, \
             nc.sbuf_tensor(f"qf{L}", [128, 256], F32) as qf, \
             nc.sbuf_tensor(f"qsqb{L}", [128, 256], BF16) as qsqb, \
             nc.sbuf_tensor(f"nrm{L}", [2, 256], F32) as nrm, \
             nc.sbuf_tensor(f"t1q{L}", [128, 256], F32) as t1q, \
             nc.sbuf_tensor(f"t1qb{L}", [128, 256], BF16) as t1qb, \
             nc.sbuf_tensor(f"im1{L}", [128, 256], F32) as im1, \
             nc.sbuf_tensor(f"im2{L}", [128, 256], F32) as im2, \
             nc.sbuf_tensor(f"vblk2{L}", [128, HG], BF16) as vblk2:

            def build_xsb(h0):
                for pt in range(8):
                    p.do("scalar", lambda e, pt=pt, h0=h0: e.copy(
                        xsb[:, pt * 1025:(pt + 1) * 1025],
                        xres[pt][:, h0:h0 + 1025]),
                        waits=(("te", p.val("te")),) if pt == 0 else (),
                        inc="se")
                p.bar()

            for part in range(2):
                for kt in range(8):
                    p.do("gpsimd", lambda e, kt=kt, part=part, L=L:
                         e.dma_start(
                             out=wqi[:, kt * 1024:(kt + 1) * 1024],
                             in_=wqkv_i[L, kt * 128:(kt + 1) * 128,
                                        part * 1024:(part + 1) * 1024]),
                         waits=(("te", p.val("te")),) if kt == 0 else (),
                         inc="gq", incby=16)
                p.bar()
                for halfw in range(2):
                    build_xsb(halfw * 1024)
                    cin = (ics if part == 0 else ick)
                    sin_ = (iss if part == 0 else isk)
                    p.do("sync", lambda e, halfw=halfw, cin=cin: e.dma_start(
                        out=tcsi[:],
                        in_=cin[:, halfw * 1024:(halfw + 1) * 1024]),
                        inc="sq", incby=16)
                    p.do("sync", lambda e, halfw=halfw, sin_=sin_: e.dma_start(
                        out=tsni[:],
                        in_=sin_[:, halfw * 1024:(halfw + 1) * 1024]),
                        inc="sq", incby=16)
                    p.bar()
                    se_prev = p.val("se")
                    ve_prev = p.val("ve")
                    qslot = {0: p.val("sq"), 1: p.val("sq")}
                    for m in range(8):
                        for wr in range(4):
                            w = halfw * 4 + wr
                            psq = PS[wr % 2]
                            for kt in range(8):
                                off = 1 if kt < 4 else 0
                                p.do("tensor",
                                     lambda e, m=m, kt=kt, wr=wr, psq=psq,
                                     off=off: e.matmul(
                                         psq[:, 0:256],
                                         wqi[:, kt * 1024 + m * 128:
                                             kt * 1024 + (m + 1) * 128],
                                         xsb[:, kt * 1025 + wr * 256 + off:
                                             kt * 1025 + wr * 256 + off + 256],
                                         start=(kt == 0), stop=(kt == 7)),
                                     waits=[("se", se_prev)] if kt == 0 else (),
                                     inc="te" if kt == 7 else None)
                            tmm = p.val("te")
                            p.do("scalar", lambda e, psq=psq: e.copy(
                                qf[:], psq[:, 0:256]),
                                waits=(("te", tmm), ("ve", ve_prev)),
                                inc="se")
                            p.do("scalar", lambda e, psq=psq: e.activation(
                                out=qsqb[:], in_=psq[:, 0:256],
                                func=AF.Square), inc="se")
                            se_prev = p.val("se")
                            p.do("tensor", lambda e: e.matmul(
                                PS[4][0:2, 0:256], t_ones2[:], qsqb[:],
                                start=True, stop=True),
                                waits=(("se", se_prev),), inc="te")
                            p.do("scalar", lambda e: e.activation(
                                out=nrm[:], in_=PS[4][0:2, 0:256],
                                func=AF.Sqrt),
                                waits=(("te", p.val("te")), ("ve", ve_prev)),
                                inc="se")
                            p.do("vector", lambda e: e.tensor_scalar(
                                out=nrm[:], in0=nrm[:], scalar1=1e-12,
                                scalar2=None, op0=ALU.max),
                                waits=(("se", p.val("se")),), inc="ve")
                            p.do("vector", lambda e: e.reciprocal(
                                nrm[:], nrm[:]),
                                waits=(("ve", p.val("ve")),), inc="ve")
                            p.do("tensor", lambda e: e.matmul(
                                PS[5][:, 0:256], t_sel2[:], nrm[:],
                                start=True, stop=True),
                                waits=(("ve", p.val("ve")),), inc="te")
                            tnb = p.val("te")
                            sc = (t_qs if part == 0 else t_ks)
                            p.do("vector", lambda e, sc=sc:
                                 e.scalar_tensor_tensor(
                                     out=t1q[:], in0=qf[:],
                                     scalar=sc[:, 0:1], in1=PS[5][:, 0:256],
                                     op0=ALU.mult, op1=ALU.mult),
                                 waits=(("te", tnb),))
                            p.do("vector", lambda e, sc=sc:
                                 e.scalar_tensor_tensor(
                                     out=t1qb[:], in0=qf[:],
                                     scalar=sc[:, 0:1], in1=PS[5][:, 0:256],
                                     op0=ALU.mult, op1=ALU.mult), inc="ve")
                            p.do("tensor", lambda e: e.matmul(
                                PS[6][:, 0:256], t_perm[:], t1qb[:],
                                start=True, stop=True),
                                waits=(("ve", p.val("ve")),), inc="te")
                            tpm = p.val("te")
                            p.do("vector", lambda e, wr=wr:
                                 e.tensor_mul(
                                     im1[:], t1q[:],
                                     tcsi[:, wr * 256:(wr + 1) * 256]),
                                 waits=(("te", tpm),))
                            p.do("vector", lambda e, wr=wr:
                                 e.tensor_mul(
                                     im2[:], tsni[:, wr * 256:(wr + 1) * 256],
                                     PS[6][:, 0:256]), inc="ve")
                            sl = wr % 2
                            p.do("vector", lambda e, sl=sl: e.tensor_add(
                                qko[:, sl * 256:(sl + 1) * 256],
                                im1[:], im2[:]),
                                waits=(("sq", qslot[sl]),
                                       ("ve", p.val("ve"))), inc="ve")
                            ve_prev = p.val("ve")
                            dscr = qscr_i if part == 0 else kscr_i
                            qslot[sl] = p.do(
                                "sync", lambda e, m=m, w=w, sl=sl,
                                dscr=dscr: e.dma_start(
                                    out=dscr[m, :, w * 256:(w + 1) * 256],
                                    in_=qko[:, sl * 256:(sl + 1) * 256]),
                                waits=(("ve", ve_prev),), inc="sq", incby=16)
                            p._wait("sync", (("sq", qslot[sl]),))
            for kt in range(8):
                p.do("gpsimd", lambda e, kt=kt, L=L: e.dma_start(
                    out=wqi[:, kt * 1024:(kt + 1) * 1024],
                    in_=wqkv_i[L, kt * 128:(kt + 1) * 128, 2048:3072]),
                    waits=(("te", p.val("te")),) if kt == 0 else (),
                    inc="gq", incby=16)
            p.bar()
            for halfw in range(2):
                build_xsb(halfw * 1024)
                for tt in range(8):
                    for kt in range(8):
                        off = 1 if kt < 4 else 0
                        p.do("tensor", lambda e, tt=tt, kt=kt, off=off:
                             e.matmul(
                                 PS[4][:, 0:512],
                                 xsb[:, kt * 1025 + off + tt * 128:
                                     kt * 1025 + off + (tt + 1) * 128],
                                 wqi[:, kt * 1024: kt * 1024 + 512],
                                 start=(kt == 0), stop=(kt == 7)))
                        p.do("tensor", lambda e, tt=tt, kt=kt, off=off:
                             e.matmul(
                                 PS[5][:, 0:512],
                                 xsb[:, kt * 1025 + off + tt * 128:
                                     kt * 1025 + off + (tt + 1) * 128],
                                 wqi[:, kt * 1024 + 512: kt * 1024 + 1024],
                                 start=(kt == 0), stop=(kt == 7)),
                             inc="te" if kt == 7 else None)
                    tv = p.val("te")
                    p.do("vector", lambda e: e.memset(
                        vblk2.rearrange("p (h c) -> p h c", h=H)[:, :, 64:65],
                        1.0), waits=(("te", tv),), inc="ve")
                    vv = p.val("ve")
                    for half, pv in ((0, PS[4]), (1, PS[5])):
                        p.do("scalar", lambda e, half=half, pv=pv: e.copy(
                            vblk2.rearrange("p (h c) -> p h c", h=H)[
                                :, half * 8:(half + 1) * 8, 0:64],
                            pv[:, 0:512].rearrange("p (h c) -> p h c", h=8)),
                            waits=(("ve", vv),), inc="se")
                    sv = p.val("se")
                    p.do("sync", lambda e, halfw=halfw, tt=tt: e.dma_start(
                        out=vscr_i[2 + halfw * 8 + tt], in_=vblk2[:]),
                        waits=(("se", sv),), inc="sq", incby=16)
                    p.bar()
            p.flush()

        dprobe('L%d-postqkv' % L)
        for m in range(8):
            mb_dual(lambda e, m=m, par=par: e.dma_start(
                out=mb_k[0, par, m], in_=kscr_i[m, :, NOWN - WIN: NOWN]),
                lambda e, m=m, par=par: e.dma_start(
                out=mb_k[1, par, m], in_=kscr_i[m, :, NOWN - WIN: NOWN]),
                inc="sq")
        for c in range(2):
            mb_dual(lambda e, par=par, c=c: e.dma_start(
                out=mb_v[0, par, c], in_=vscr_i[16 + c]),
                lambda e, par=par, c=c: e.dma_start(
                out=mb_v[1, par, c], in_=vscr_i[16 + c]),
                inc="sq")
        k_kv = mb_flag(waits=(("sq", p.val("sq")),))
        k_ffn = [None]

        with nc.sbuf_tensor(f"woib{L}", [128, 8 * 1024], BF16) as woib, \
             nc.sbuf_tensor(f"vwin{L}", [128, 4 * HG], BF16) as vwin, \
             nc.sbuf_tensor(f"kwin{L}", [128, 8 * 512], BF16) as kwin, \
             nc.sbuf_tensor(f"qwin{L}", [128, 8 * 256], BF16) as qwin, \
             nc.sbuf_tensor(f"nmx{L}", [128, 2], F32) as nmx, \
             nc.sbuf_tensor(f"nmxb{L}", [128, 2], BF16) as nmxb, \
             nc.sbuf_tensor(f"ptile{L}", [128, 4 * 256], BF16) as ptile, \
             nc.sbuf_tensor(f"orec{L}", [128, 256], F32) as orec, \
             nc.sbuf_tensor(f"orecb{L}", [128, 256], BF16) as orecb, \
             nc.sbuf_tensor(f"obuf2{L}", [128, 8 * 256], BF16) as obuf2:

            for kt in range(8):
                p.do("gpsimd", lambda e, kt=kt, L=L: e.dma_start(
                    out=woib[:, kt * 1024:(kt + 1) * 1024],
                    in_=wo_i[L, kt * 128:(kt + 1) * 128, :]),
                    inc="gq", incby=16)
            p.bar()
            for w in list(range(1, NWIN)) + [0]:
                dprobe('L%d-w%d' % (L, w))
                tprev = p.val("te")
                p.do("sync", lambda e, w=w: e.dma_start(
                    out=qwin.rearrange("p (a c) -> p a c", a=8),
                    in_=qscr_i[:, :, w * 256:(w + 1) * 256].rearrange(
                        "a p c -> p a c")),
                    waits=(("te", tprev),), inc="sq", incby=16)
                if w == 0:
                    mb_poll(k_kv)
                    for m in range(8):
                        mb_dual(lambda e, par=par, m=m: e.dma_start(
                            out=kwin[:, m * 512: m * 512 + WIN],
                            in_=mb_k[1, par, m]),
                            lambda e, par=par, m=m: e.dma_start(
                            out=kwin[:, m * 512: m * 512 + WIN],
                            in_=mb_k[0, par, m]),
                            inc="sq")
                    p.do("sync", lambda e: e.dma_start(
                        out=kwin.rearrange("p (a c) -> p a c", a=8)[
                            :, :, WIN:512],
                        in_=kscr_i[:, :, 0:WIN].rearrange("a p c -> p a c")),
                        inc="sq", incby=16)
                    for c in range(2):
                        mb_dual(lambda e, par=par, c=c: e.dma_start(
                            out=vwin[:, c * HG:(c + 1) * HG],
                            in_=mb_v[1, par, c]),
                            lambda e, par=par, c=c: e.dma_start(
                            out=vwin[:, c * HG:(c + 1) * HG],
                            in_=mb_v[0, par, c]),
                            inc="sq")
                    sv = p.val("sq")
                    for m in range(8):
                        p.do("vector", lambda e, m=m: e.tensor_scalar(
                            out=kwin[:, m * 512: m * 512 + WIN],
                            in0=kwin[:, m * 512: m * 512 + WIN],
                            scalar1=t_pad[:, 0:1], scalar2=None,
                            op0=ALU.mult),
                            waits=(("sq", sv),) if m == 0 else ())
                    for c in range(2):
                        p.do("vector", lambda e, c=c: e.tensor_scalar(
                            out=vwin[:, c * HG:(c + 1) * HG],
                            in0=vwin[:, c * HG:(c + 1) * HG],
                            scalar1=t_pad[:, 0:1], scalar2=None,
                            op0=ALU.mult))
                    p.do("vector", lambda e: e.nop(), inc="ve")
                    p.do("gpsimd", lambda e, w=w: e.dma_start(
                        out=vwin.rearrange("p (a c) -> p a c", a=4)[:, 2:4],
                        in_=vscr_i[2: 4].rearrange("a p c -> p a c")),
                        inc="gq", incby=16)
                else:
                    p.do("sync", lambda e, w=w: e.dma_start(
                        out=kwin.rearrange("p (a c) -> p a c", a=8),
                        in_=kscr_i[:, :, (w - 1) * 256:(w + 1) * 256]
                        .rearrange("a p c -> p a c")), inc="sq", incby=16)
                    p.do("gpsimd", lambda e, w=w: e.dma_start(
                        out=vwin.rearrange("p (a c) -> p a c", a=4),
                        in_=vscr_i[2 * w: 4 + 2 * w].rearrange(
                            "a p c -> p a c")), inc="gq", incby=16)
                p.bar()
                mvar = 0 if w == 0 else 1
                se_prev = p.val("se")
                ve_prev = p.val("ve")
                te_prev = p.val("te")
                for hp in range(8):
                    for hh in range(2):
                        h = 2 * hp + hh
                        r0, r1 = hh * 64, (hh + 1) * 64
                        for qt in range(2):
                            nk = 384 if qt == 0 else 512
                            psq = PS[qt]
                            p.do("tensor", lambda e, hp=hp, r0=r0, r1=r1,
                                 qt=qt, nk=nk, psq=psq: e.matmul(
                                     psq[:, 0:nk],
                                     qwin[r0:r1, hp * 256 + qt * 128:
                                          hp * 256 + (qt + 1) * 128],
                                     kwin[r0:r1, hp * 512: hp * 512 + nk],
                                     start=True, stop=True),
                                 waits=[("se", se_prev), ("ve", ve_prev)]
                                 if qt == 0 else (), inc="te")
                            tmm = p.val("te")
                            p.do("vector", lambda e, nk=nk, psq=psq:
                                 e.tensor_add(
                                     psq[:, nk - 128:nk],
                                     psq[:, nk - 128:nk], t_m1[:]),
                                 waits=(("te", tmm), ("te", te_prev)),
                                 inc="ve")
                            if w == 0:
                                p.do("vector", lambda e, nk=nk, psq=psq:
                                     e.tensor_add(
                                         psq[:, 0:256], psq[:, 0:256],
                                         t_w0pm[:, 0:256]), inc="ve")
                            p.do("vector", lambda e, qt=qt, nk=nk, psq=psq:
                                 e.reduce_max(
                                     nmx[:, qt:qt + 1], psq[:, 0:nk],
                                     axis=AX.X, negate=True),
                                 waits=(("ve", p.val("ve")),), inc="ve")
                        p.do("vector", lambda e: e.tensor_copy(
                            nmxb[:], nmx[:]),
                            waits=(("ve", p.val("ve")),), inc="ve")
                        vv = p.val("ve")
                        for qt in range(2):
                            p.do("tensor", lambda e, qt=qt: e.matmul(
                                PS[2][64:65, qt * 128:(qt + 1) * 128],
                                nmxb[:, qt:qt + 1], t_identb[:],
                                start=True, stop=True,
                                tile_position=(0, 64)),
                                waits=(("ve", vv),) if qt == 0
                                else (("se", p.val("se")),),
                                inc="te")
                            tw = p.val("te")
                            p.do("scalar", lambda e, qt=qt: e.mul(
                                t_stageb[64:65, qt * 128:(qt + 1) * 128],
                                PS[2][64:65, qt * 128:(qt + 1) * 128], 1.0),
                                waits=(("te", tw),), inc="se")
                        sv = p.val("se")
                        expd, avd = {}, {}
                        for kt in range(4):
                            psS = PS[kt % 2]
                            wts = [("se", sv)]
                            if kt >= 2:
                                wts.append(("se", expd[kt - 2]))
                            p.do("tensor", lambda e, hp=hp, r0=r0, r1=r1,
                                 kt=kt, psS=psS: e.matmul(
                                     psS[:, 0:256],
                                     kwin[r0:r1, hp * 512 + kt * 128:
                                          hp * 512 + (kt + 1) * 128],
                                     qwin[r0:r1, hp * 256:
                                          hp * 256 + 256],
                                     start=True, stop=False), waits=wts)
                            p.do("tensor", lambda e, psS=psS: e.matmul(
                                psS[:, 0:256], t_onesb[64:65, 0:128],
                                t_stageb[64:65, 0:256],
                                start=False, stop=True), inc="te")
                            tmm = p.val("te")
                            ew = [("te", tmm)]
                            if kt < 2:
                                p.do("vector", lambda e, kt=kt, mvar=mvar,
                                     psS=psS: e.tensor_add(
                                         psS[:, 0:256], psS[:, 0:256],
                                         t_m01n[:, (mvar * 2 + kt) * 256:
                                                (mvar * 2 + kt + 1) * 256]),
                                     waits=(("te", tmm),), inc="ve")
                                ew = [("ve", p.val("ve"))]
                            if kt == 2:
                                p.do("vector", lambda e, psS=psS:
                                     e.tensor_add(psS[:, 0:128],
                                                  psS[:, 0:128], t_m2T[:]),
                                     waits=(("te", tmm),), inc="ve")
                                ew = [("ve", p.val("ve"))]
                            if kt == 3:
                                p.do("vector", lambda e, psS=psS:
                                     e.tensor_add(psS[:, 128:256],
                                                  psS[:, 128:256], t_m2T[:]),
                                     waits=(("te", tmm),), inc="ve")
                                ew = [("ve", p.val("ve"))]
                            if kt >= 2:
                                ew.append(("te", avd[kt - 2]))
                            psl = ptile[:, kt * 256:(kt + 1) * 256]
                            if kt < 3:
                                p.do("scalar", lambda e, psl=psl, psS=psS:
                                     e.activation(out=psl, in_=psS[:, 0:256],
                                                  func=AF.Exp, scale=8.0),
                                     waits=ew, inc="se")
                            else:
                                p.do("scalar", lambda e, psl=psl, psS=psS:
                                     e.activation(out=psl[:, 128:256],
                                                  in_=psS[:, 128:256],
                                                  func=AF.Exp, scale=8.0),
                                     waits=ew, inc="se")
                                p.do("vector", lambda e, psl=psl: e.memset(
                                    psl[:, 0:128], 0.0), inc="ve")
                            expd[kt] = p.val("se")
                            mwts = [("se", expd[kt])]
                            if kt == 3:
                                mwts.append(("ve", p.val("ve")))
                            p.do("tensor", lambda e, kt=kt, hp=hp, hh=hh,
                                 psl=psl: e.matmul(
                                     PS[4][0:65, 0:256],
                                     vwin[:, kt * HG + (2 * hp + hh) * 65:
                                          kt * HG + (2 * hp + hh + 1) * 65],
                                     psl, start=(kt == 0), stop=(kt == 3)),
                                 waits=mwts, inc="te")
                            avd[kt] = p.val("te")
                        tw = p.val("te")

                        def _recip_i(e):
                            with nc.allow_low_precision(
                                    reason="recip feeds bf16 matmul; same "
                                           "precision as prior copy-cast"):
                                e.reciprocal(orecb[64:65, 0:256],
                                             PS[4][64:65, 0:256])
                        p.do("vector", _recip_i,
                             waits=(("te", tw),), inc="ve")
                        vv = p.val("ve")
                        p.do("tensor", lambda e: e.matmul(
                            PS[6][0:64, 0:256], t_onesb[64:65, 0:64],
                            orecb[64:65, 0:256], start=True, stop=True),
                            waits=(("ve", vv), ("se", p.val("se"))),
                            inc="te")
                        tw2 = p.val("te")
                        p.do("scalar", lambda e: e.copy(
                            orec[0:64, 0:256], PS[6][0:64, 0:256]),
                            waits=(("te", tw2),), inc="se")
                        sv2 = p.val("se")
                        p.do("vector", lambda e, hp=hp, hh=hh: e.tensor_mul(
                            obuf2[hh * 64:(hh + 1) * 64,
                                  hp * 256:(hp + 1) * 256],
                            PS[4][0:64, 0:256], orec[0:64, 0:256]),
                            waits=(("se", sv2),), inc="ve")
                        se_prev = p.val("se")
                        ve_prev = p.val("ve")
                        te_prev = p.val("te")
                for grp in range(2):
                    for mi in range(4):
                        mt = grp * 4 + mi
                        for ktp in range(8):
                            p.do("tensor", lambda e, mt=mt, ktp=ktp, mi=mi:
                                 e.matmul(
                                     PS[4 + mi][:, 0:256],
                                     woib[:, ktp * 1024 + mt * 128:
                                          ktp * 1024 + (mt + 1) * 128],
                                     obuf2[:, ktp * 256:(ktp + 1) * 256],
                                     start=(ktp == 0), stop=(ktp == 7)),
                                 waits=[("ve", ve_prev)]
                                 if (mi == 0 and ktp == 0 and grp == 0)
                                 else (), inc="te" if ktp == 7 else None)
                    tw = p.val("te")
                    for mi in range(4):
                        mt = grp * 4 + mi
                        p.do("vector", lambda e, mt=mt, mi=mi, w=w:
                             e.tensor_add(
                                 xres[mt][:, 1 + w * 256: 1 + (w + 1) * 256],
                                 xres[mt][:, 1 + w * 256: 1 + (w + 1) * 256],
                                 PS[4 + mi][:, 0:256]),
                             waits=(("te", tw),), inc="ve")
                    p.bar()
                if w == NWIN - 1:
                    k_ffn[0] = publish_xcol(par, (("ve", p.val("ve")),))
            p.flush()

        if STOP_AFTER == ("l%da" % L):
            return dump_xres_and_stop()

        recv_xcol(k_ffn[0], par)
        with nc.sbuf_tensor(f"xnb{L}", [128, 8 * 2048], BF16) as xnb, \
             nc.sbuf_tensor(f"grow{L}", [1, D], F32) as grow, \
             nc.sbuf_tensor(f"brow{L}", [1, D], F32) as brow, \
             nc.sbuf_tensor(f"b2row{L}", [1, D], BF16) as b2row, \
             nc.sbuf_tensor(f"b1ac{L}", [128, NHT], F32) as b1ac, \
             nc.sbuf_tensor(f"b1gc{L}", [128, NHT], F32) as b1gc, \
             nc.sbuf_tensor(f"w1at{L}", [128, 8 * 128], BF16) as w1at, \
             nc.sbuf_tensor(f"w1gt{L}", [128, 8 * 128], BF16) as w1gt, \
             nc.sbuf_tensor(f"gel{L}", [128, 512], BF16) as gel, \
             nc.sbuf_tensor(f"lnX{L}", [128, 512], F32) as lnX, \
             nc.sbuf_tensor(f"th{L}", [128, HHALF * 2048], BF16) as th, \
             nc.sbuf_tensor(f"w2t{L}", [128, HHALF * 1024], BF16) as w2t:

            p.do("sync", lambda e, L=L: e.dma_start(
                out=grow[:], in_=lng[L:L + 1, :]), inc="sq", incby=16)
            p.do("sync", lambda e, L=L: e.dma_start(
                out=brow[:], in_=lnb[L:L + 1, :]), inc="sq", incby=16)
            p.do("gpsimd", lambda e, L=L: e.dma_start(
                out=b2row[:], in_=b2_i[L:L + 1, :]), inc="gq", incby=16)
            p.do("sync", lambda e, L=L: e.dma_start(
                out=b1ac[:], in_=b1a_i[L]), inc="sq", incby=16)
            p.do("sync", lambda e, L=L: e.dma_start(
                out=b1gc[:], in_=b1g_i[L]), inc="sq", incby=16)
            p.bar()

            for qb in range(NQB):
                shsrc = []
                for pt in range(8):
                    off = 1 if pt < 4 else 0
                    shsrc.append(xres[pt][:, off + qb * QB:
                                          off + (qb + 1) * QB])
                emit_ln_stats(shsrc)
                for pt in range(8):
                    p.do("tensor", lambda e, pt=pt: e.matmul(
                        PS[2][:, 0:512], grow[0:1, pt * 128:(pt + 1) * 128],
                        t_rstd[:], start=True, stop=True),
                        waits=(("ve", p.val("ve")),) if pt > 0 else ())
                    p.do("tensor", lambda e, pt=pt: e.matmul(
                        PS[3][:, 0:512], grow[0:1, pt * 128:(pt + 1) * 128],
                        t_mu[:], start=True, stop=False))
                    p.do("tensor", lambda e, pt=pt: e.matmul(
                        PS[3][:, 0:512], brow[0:1, pt * 128:(pt + 1) * 128],
                        t_onesf[0:1, 0:512], start=False, stop=True),
                        inc="te")
                    tw = p.val("te")
                    p.do("vector", lambda e, pt=pt, sl=shsrc[pt]:
                         e.tensor_mul(lnX[:], sl, PS[2][:, 0:512]),
                         waits=(("te", tw),), inc="ve")
                    p.do("vector", lambda e, pt=pt, qb=qb: e.tensor_add(
                        xnb[:, pt * 2048 + qb * 512:
                            pt * 2048 + (qb + 1) * 512],
                        lnX[:], PS[3][:, 0:512]),
                        waits=(("ve", p.val("ve")),), inc="ve")
                p._wait("tensor", (("ve", p.val("ve")),))

            for hf in range(2):
                for htl in range(HHALF):
                    ht = hf * HHALF + htl
                    p.do("gpsimd", lambda e, ht=ht, L=L: e.dma_start(
                        out=w1at.rearrange("p (k c) -> p k c", k=8),
                        in_=W1a[L, :, ht * 128:(ht + 1) * 128].rearrange(
                            "(k p) c -> p k c", p=128)), inc="gq", incby=16)
                    p.do("gpsimd", lambda e, ht=ht, L=L: e.dma_start(
                        out=w1gt.rearrange("p (k c) -> p k c", k=8),
                        in_=W1g[L, :, ht * 128:(ht + 1) * 128].rearrange(
                            "(k p) c -> p k c", p=128)), inc="gq", incby=16)
                    p.bar()
                    for ch in range(4):
                        chw = [("se", p.val("se")), ("ve", p.val("ve"))] \
                            if ch > 0 else ()
                        for kt in range(8):
                            p.do("tensor", lambda e, kt=kt, ch=ch: e.matmul(
                                PS[0][:, 0:512],
                                w1at[:, kt * 128:(kt + 1) * 128],
                                xnb[:, kt * 2048 + ch * 512:
                                    kt * 2048 + (ch + 1) * 512],
                                start=(kt == 0), stop=(kt == 7)),
                                waits=chw if kt == 0 else ())
                            p.do("tensor", lambda e, kt=kt, ch=ch: e.matmul(
                                PS[1][:, 0:512],
                                w1gt[:, kt * 128:(kt + 1) * 128],
                                xnb[:, kt * 2048 + ch * 512:
                                    kt * 2048 + (ch + 1) * 512],
                                start=(kt == 0), stop=(kt == 7)),
                                inc="te" if kt == 7 else None)
                        tw = p.val("te")
                        p.do("scalar", lambda e, ht=ht: e.activation(
                            out=gel[:], in_=PS[1][:, 0:512], func=AF.Gelu,
                            bias=b1gc[:, ht:ht + 1]),
                            waits=(("te", tw),), inc="se")
                        sv = p.val("se")
                        p.do("vector", lambda e, htl=htl, ch=ch, ht=ht:
                             e.scalar_tensor_tensor(
                                 out=th[:, htl * 2048 + ch * 512:
                                        htl * 2048 + (ch + 1) * 512],
                                 in0=PS[0][:, 0:512],
                                 scalar=b1ac[:, ht:ht + 1], in1=gel[:],
                                 op0=ALU.add, op1=ALU.mult),
                             waits=(("se", sv),), inc="ve")
                    p.bar()
                for ktl in range(HHALF):
                    kt = hf * HHALF + ktl
                    p.do("gpsimd", lambda e, kt=kt, ktl=ktl, L=L: e.dma_start(
                        out=w2t[:, ktl * 1024:(ktl + 1) * 1024],
                        in_=W2p[L, kt * 128:(kt + 1) * 128, :]),
                        inc="gq", incby=16)
                p.bar()
                for ch in range(4):
                    for grp in range(2):
                        gw = [("ve", p.val("ve"))] if (ch or grp) else ()
                        for mi in range(4):
                            mt = grp * 4 + mi
                            for ktl in range(HHALF):
                                last = (ktl == HHALF - 1) and (hf == 1)
                                p.do("tensor", lambda e, mt=mt, ktl=ktl,
                                     ch=ch, mi=mi, last=last: e.matmul(
                                         PS[4 + mi][:, 0:512],
                                         w2t[:, ktl * 1024 + mt * 128:
                                             ktl * 1024 + (mt + 1) * 128],
                                         th[:, ktl * 2048 + ch * 512:
                                            ktl * 2048 + (ch + 1) * 512],
                                         start=(ktl == 0), stop=last),
                                     waits=gw if (mi == 0 and ktl == 0)
                                     else (), inc="te" if last else None)
                            if hf == 0:
                                p.do("tensor", lambda e, mt=mt, mi=mi:
                                     e.matmul(
                                         PS[4 + mi][:, 0:512],
                                         b2row[0:1, mt * 128:(mt + 1) * 128],
                                         t_onesb[0:1, 0:512],
                                         start=False, stop=True), inc="te")
                        tw = p.val("te")
                        for mi in range(4):
                            mt = grp * 4 + mi
                            p.do("vector", lambda e, mt=mt, mi=mi, ch=ch:
                                 e.tensor_add(
                                     xres[mt][:, 1 + ch * 512:
                                              1 + (ch + 1) * 512],
                                     xres[mt][:, 1 + ch * 512:
                                              1 + (ch + 1) * 512],
                                     PS[4 + mi][:, 0:512]),
                                 waits=(("te", tw),), inc="ve")
                p.bar()
            p.flush()

        if STOP_AFTER == ("l%d" % L):
            return dump_xres_and_stop()
        if ("xl%d" % L) in dbg:
            dbg_dump("xl%d" % L,
                     [(xres[pt][:, 1:1 + NOWN],
                       dbg["xl%d" % L][pt * 128:(pt + 1) * 128, :])
                      for pt in range(8)])

    with nc.sbuf_tensor("x1b", [128, 8 * 512], F32) as x1b, \
         nc.sbuf_tensor("t_ln4", [128, 8 * 512], F32) as t_ln, \
         nc.sbuf_tensor("ytm", [128, 1024], F32) as ytm:
        for qb in range(NQB):
            p.do("sync", lambda e, qb=qb: e.dma_start(
                out=x1b.rearrange("p (a c) -> p a c", a=8),
                in_=x1scr[:, :, qb * QB:(qb + 1) * QB].rearrange(
                    "a p c -> p a c")), inc="sq", incby=16)
            p.bar()
            for pt in range(8):
                p.do("vector", lambda e, pt=pt, qb=qb: e.tensor_add(
                    t_ln[:, pt * 512:(pt + 1) * 512],
                    xres[pt][:, 1 + qb * QB: 1 + (qb + 1) * QB],
                    x1b[:, pt * 512:(pt + 1) * 512]), inc="ve")
            p.bar()
            emit_ln_stats([t_ln[:, pt * 512:(pt + 1) * 512]
                           for pt in range(8)])
            p.do("tensor", lambda e: e.matmul(
                PS[2][:, 0:512], t_onesf[0:1, 0:128], t_rstd[:],
                start=True, stop=True))
            p.do("tensor", lambda e: e.matmul(
                PS[3][:, 0:512], t_onesf[0:1, 0:128], t_mu[:],
                start=True, stop=True), inc="te")
            tw = p.val("te")
            for pt in range(8):
                p.do("vector", lambda e, pt=pt: e.tensor_mul(
                    t_ln[:, pt * 512:(pt + 1) * 512],
                    t_ln[:, pt * 512:(pt + 1) * 512], PS[2][:, 0:512]),
                    waits=(("te", tw),) if pt == 0 else (),
                    inc="ve" if pt == 7 else None)
            mv = p.val("ve")
            for pt in range(8):
                p.do("vector", lambda e, pt=pt: e.tensor_add(
                    t_ln[:, pt * 512:(pt + 1) * 512],
                    t_ln[:, pt * 512:(pt + 1) * 512], PS[3][:, 0:512]),
                    waits=(("ve", mv),) if pt == 0 else (), inc="ve")
            p.bar()
            for tt in range(4):
                for pt in range(8):
                    pb = PS[pt % 2]
                    wl = [("se", p.val("se") - 1)] if (tt or pt) else ()
                    p.do("tensor", lambda e, tt=tt, pt=pt, pb=pb: e.transpose(
                        pb[:, 0:128],
                        t_ln[:, pt * 512 + tt * 128:
                             pt * 512 + (tt + 1) * 128],
                        t_ident[:]), waits=wl, inc="te")
                    tw = p.val("te")
                    p.do("scalar", lambda e, pt=pt, pb=pb: e.copy(
                        ytm[:, pt * 128:(pt + 1) * 128], pb[:, 0:128]),
                        waits=(("te", tw),), inc="se")
                sv = p.val("se")
                p.do("sync", lambda e, qb=qb, tt=tt: e.dma_start(
                    out=y[qb * QB + tt * 128: qb * QB + (tt + 1) * 128, :],
                    in_=ytm[:]), waits=(("se", sv),), inc="sq", incby=16)
                p.bar()
        p.bar()
        p.flush()

    st.close()
    return nc



_CACHE = {}


def kernel(**inputs):
    if "nc" not in _CACHE:
        _CACHE["nc"] = build_program(debug=())
    nc = _CACHE["nc"]

    const = host_const_tables()
    wts = host_weight_layouts(inputs)
    xall = np.asarray(inputs["L0_attn_x"], dtype=np.float32)
    nonce = np.random.randint(1, 2 ** 31 - 100, dtype=np.uint32)
    fv = (nonce + np.arange(1, 65, dtype=np.uint32)).reshape(1, 64)

    in_maps = []
    for core in range(NCORE):
        be, hf = core // 2, core % 2
        pc = host_percore_tables(core, const)
        m = {
            "xfull": np.ascontiguousarray(xall[be]),
            "xown": np.ascontiguousarray(xall[be, hf * NOWN:(hf + 1) * NOWN]),
            "fvals": fv,
        }
        for k in ("ocos", "osin", "permB", "ident", "identb", "onesf",
                  "onesb", "ones2", "sel2", "posrel64", "m1stair",
                  "m2stairT"):
            m[k] = const[k]
        for k in ("ocos_q", "osin_q", "ics", "iss", "ick", "isk", "poskrel",
                  "omaskflag", "ocausal", "w0prevmask", "mask01prev",
                  "mask01neg",
                  "padmask", "slot"):
            m[k] = pc[k]
        for k in ("wqkv_o", "wo_o", "logxi_r", "ebits_r", "wqkv_i", "wo_i",
                  "qs2", "ks2", "lng", "lnb", "W1a", "W1g", "b1a", "b1g",
                  "W2p", "b2"):
            m[k] = wts[k]
        in_maps.append(m)

    res = run_bass_kernel_spmd(nc, in_maps, core_ids=list(range(NCORE)))
    out = np.empty((B, N, D), dtype=np.float32)
    for core in range(NCORE):
        be, hf = core // 2, core % 2
        out[be, hf * NOWN:(hf + 1) * NOWN] = res.results[core]["y"]
    return out
